# version 21
# baseline (speedup 1.0000x reference)
import numpy as np
import concourse.bass as bass
import concourse.mybir as mybir
from concourse.bass_utils import run_bass_kernel_spmd
from contextlib import ExitStack

F32 = mybir.dt.float32
BF16 = mybir.dt.bfloat16
U32 = mybir.dt.uint32
ALU = mybir.AluOpType
AF = mybir.ActivationFunctionType
AX = mybir.AxisListType

MAGIC = float(np.float32(0.75 * 2 ** 23))
SIXTH = float(np.float32(1.0 / 6.0))
N_CORES = 8


class Sem:
    def __init__(self, es, nc, name):
        self.h = es.enter_context(nc.semaphore(name))
        self.n = 0


class Prog:
    def __init__(self):
        self.ops = {k: [] for k in ("sync", "vector", "scalar", "tensor", "gpsimd")}

    def emit(self, eng, fn, waits=(), inc=None):
        ws = [(s.h, int(v)) for (s, v) in waits if v > 0]
        k_inc = None
        if inc is not None:
            s, k = inc
            s.n += k
            k_inc = (s.h, k)

        def run(e, fn=fn, ws=ws, k_inc=k_inc):
            for h, v in ws:
                e.wait_ge(h, v)
            inst = fn(e)
            if k_inc is not None:
                inst.then_inc(*k_inc)

        self.ops[eng].append(run)

    def wait(self, eng, waits):
        ws = [(s.h, int(v)) for (s, v) in waits if v > 0]
        if not ws:
            return

        def run(e, ws=ws):
            for h, v in ws:
                e.wait_ge(h, v)

        self.ops[eng].append(run)


def build_core_program(nc, S=2048, NB=4, QC=None):
    D, DH, H, NP = 1024, 64, 16, 8
    NCD = D // 128
    I = 2816
    NI = I // 128
    ICH = 512
    NT = S // 128
    if QC is None:
        QC = min(1024, S)
    NQC = S // QC

    x_in = nc.dram_tensor("x", [NB * S, D], F32, kind="ExternalInput").ap()
    wqt_in = nc.dram_tensor("wqt", [D, D], F32, kind="ExternalInput").ap()
    wkt_in = nc.dram_tensor("wkt", [D, D], F32, kind="ExternalInput").ap()
    wvt_in = nc.dram_tensor("wvt", [D, D], F32, kind="ExternalInput").ap()
    wot_in = nc.dram_tensor("wot", [D, D], F32, kind="ExternalInput").ap()
    wg_in = nc.dram_tensor("w_gate", [I, D], F32, kind="ExternalInput").ap()
    wu_in = nc.dram_tensor("w_up", [I, D], F32, kind="ExternalInput").ap()
    wd_in = nc.dram_tensor("w_down", [D, I], F32, kind="ExternalInput").ap()
    out_hbm = nc.dram_tensor("out", [NB * S, D], F32, kind="ExternalOutput").ap()

    attn_scr = nc.dram_tensor("attn_scr", [NB * S, D], F32, kind="Internal").ap()
    rz_d = nc.dram_tensor("rz_d", [H, S], F32, kind="Internal").ap()
    wq_d = nc.dram_tensor("wq_d", [128, NCD, D], BF16, kind="Internal").ap()
    wk_d = nc.dram_tensor("wk_d", [128, NCD, D], BF16, kind="Internal").ap()
    wv_d = nc.dram_tensor("wv_d", [128, NCD, D], BF16, kind="Internal").ap()
    wo_d = nc.dram_tensor("wo_d", [128, NCD, D], BF16, kind="Internal").ap()
    wg_d = nc.dram_tensor("wg_d", [128, NCD, I], BF16, kind="Internal").ap()
    wu_d = nc.dram_tensor("wu_d", [128, NCD, I], BF16, kind="Internal").ap()
    wd_d = nc.dram_tensor("wd_d", [128, NI, D], BF16, kind="Internal").ap()

    es = ExitStack()
    with es:
        HEAP_BYTES = 211968
        heap = es.enter_context(nc.sbuf_tensor("heap", [128, HEAP_BYTES // 4], F32))
        idn = es.enter_context(nc.sbuf_tensor("idn", [128, 128], BF16))
        tri = es.enter_context(nc.sbuf_tensor("tri", [128, 128], BF16))
        ceps = es.enter_context(nc.sbuf_tensor("ceps", [128, 1], F32))

        def vf(off, cols, parts=128):
            assert off % 4 == 0 and off + 4 * cols <= HEAP_BYTES, (off, cols)
            return heap[0:parts, off // 4: off // 4 + cols]

        def vb(off, cols):
            assert off % 4 == 0 and off + 2 * cols <= HEAP_BYTES, (off, cols)
            n4 = (cols + 1) // 2
            return heap[:, off // 4: off // 4 + n4].bitcast(BF16)[:, 0:cols]

        SZ_T = 2 * NCD * S
        XNT = 0
        XNQT = SZ_T
        WVT = 2 * SZ_T
        WA = WVT + 2 * NCD * D
        ATNT = WVT
        QT = WA + 2 * NCD * D
        KT = QT + 2 * NP * S
        VST = KT + 2 * NP * S
        VST_END = VST + 2 * NT * 16 * 65

        F_WD = WVT
        F_H = F_WD + 2 * NI * D
        F_WG = F_H + 4 * I
        F_WU = F_WG + 2 * NCD * I
        F_END = F_WU + 2 * NCD * I
        assert F_END <= HEAP_BYTES, (F_END,)

        extra = [max(VST_END, F_END)]

        def alloc_extra(size):
            off = (extra[0] + 7) & ~7
            extra[0] = off + size
            assert extra[0] <= HEAP_BYTES, (off, size)
            return off

        SH1 = 4 * S + 4 * S + 6144 + 4096 + 4096
        SH3 = 31744
        if SH1 <= SZ_T and SH3 <= VST - QT:
            tail_base = 0
            fscr_base = 0
            ln_base = QT
        else:
            shared = alloc_extra(33000)
            tail_base = fscr_base = ln_base = shared

        RZBC = tail_base
        RZBC2 = RZBC + 4 * S
        EXPB_BASE = RZBC2 + 4 * S
        OT0 = EXPB_BASE + 6144
        OT1 = OT0 + 4096
        ZBUF = OT0

        if QT + 2 * NCD * D <= VST:
            WOT = QT
        else:
            WOT = alloc_extra(2 * NCD * D)

        F_XT = [fscr_base, fscr_base + 4096]
        F_AT = fscr_base + 8192
        F_HQT = fscr_base + 12288
        F_HQ = F_HQT + 2 * NI * 128
        F_U = [F_HQ + 2 * I, F_HQ + 2 * I + 2048]
        F_M = [F_U[1] + 2048, F_U[1] + 4096]
        F_QS = [F_M[1] + 2048, F_M[1] + 2560]
        assert F_QS[1] + 512 <= fscr_base + 33000, (F_QS,)

        L_X = [ln_base, ln_base + 4096]
        L_XN = [ln_base + 8192, ln_base + 12288]
        L_U = [ln_base + 16384, ln_base + 18432]
        L_M = [ln_base + 20480, ln_base + 22528]
        L_XNB = ln_base + 24576
        L_XNQ = [ln_base + 26624, ln_base + 28672]
        L_QS = [ln_base + 30720, ln_base + 31104]
        STATS = [ln_base + 31488, ln_base + 31616]

        wbase = F_WG
        W_STG = [wbase, wbase + 4096, wbase + 8192]
        W_QBF = [wbase + 12288, wbase + 14336]
        W_U = [wbase + 16384, wbase + 20480]
        W_M = [wbase + 24576, wbase + 28672]
        W_ST2 = [wbase + 32768, wbase + 34816]
        W_QS = [wbase + 36864, wbase + 37632]
        assert W_QS[1] + 768 <= HEAP_BYTES, (wbase,)



        ps_a = es.enter_context(nc.psum_tensor("ps_a", [128, 1024], F32))
        ps_b = es.enter_context(nc.psum_tensor("ps_b", [128, 1024], F32))
        ps_c = es.enter_context(nc.psum_tensor("ps_c", [128, 1024], F32))
        ps_t = es.enter_context(nc.psum_tensor("ps_t", [128, 1024], BF16))

        s_x = Sem(es, nc, "s_x")
        s_w = Sem(es, nc, "s_w")
        s_sc = Sem(es, nc, "s_sc")
        s_sc2 = Sem(es, nc, "s_sc2")
        s_zg = [Sem(es, nc, "s_zg0"), Sem(es, nc, "s_zg1")]
        s_st = Sem(es, nc, "s_st")
        s_ws = Sem(es, nc, "s_ws")
        s_pe = Sem(es, nc, "s_pe")
        s_ve = Sem(es, nc, "s_ve")
        s_ac = Sem(es, nc, "s_ac")
        s_po = Sem(es, nc, "s_po")
        s_wl = [Sem(es, nc, "s_wl0"), Sem(es, nc, "s_wl1"), Sem(es, nc, "s_wl2")]
        s_ws2 = [Sem(es, nc, "s_ws2a"), Sem(es, nc, "s_ws2b")]
        ALL = (s_x, s_w, s_sc, s_sc2, s_zg[0], s_zg[1], s_st, s_ws, s_pe, s_ve, s_ac, s_po,
               s_wl[0], s_wl[1], s_wl[2], s_ws2[0], s_ws2[1])

        P = Prog()

        def barrier(engines=("sync", "vector", "scalar", "tensor", "gpsimd")):
            snap = [(s, s.n) for s in ALL]
            for e in engines:
                P.wait(e, snap)

        P.emit("gpsimd", lambda e: e.memset(idn[:], 0.0), inc=(s_po, 1))
        P.emit("gpsimd", lambda e: e.memset(tri[:], 0.0), inc=(s_po, 1))
        P.emit("gpsimd", lambda e: e.affine_select(
            out=idn[:], in_=idn[:], pattern=[[-1, 128]], compare_op=ALU.not_equal,
            fill=1.0, base=0, channel_multiplier=1),
            waits=[(s_po, 2)], inc=(s_po, 1))
        P.emit("gpsimd", lambda e: e.affine_select(
            out=tri[:], in_=tri[:], pattern=[[-1, 128]], compare_op=ALU.is_gt,
            fill=1.0, base=0, channel_multiplier=1),
            waits=[(s_po, 2)], inc=(s_po, 1))
        P.emit("gpsimd", lambda e: e.memset(ceps[:], 1e-5), inc=(s_po, 1))
        po_const = s_po.n

        def quant_pre(src_f32, u_off, m_off, qs_off, F):
            nb = F // 16
            u, M = vf(u_off, F), vf(m_off, F)
            m_, sc_, r_ = vf(qs_off, nb), vf(qs_off + 4 * nb, nb), vf(qs_off + 8 * nb, nb)
            src3 = src_f32.rearrange("p (b w) -> p b w", w=16)
            P.emit("vector", lambda e: e.tensor_reduce(
                m_, src3, axis=AX.X, op=ALU.max, apply_absolute_value=True),
                waits=[(s_ve, s_ve.n)], inc=(s_ve, 1))
            P.emit("vector", lambda e: e.tensor_scalar(
                sc_, m_, SIXTH, 1e-12, op0=ALU.mult, op1=ALU.max),
                waits=[(s_ve, s_ve.n)], inc=(s_ve, 1))
            P.emit("vector", lambda e: e.reciprocal(r_, sc_),
                   waits=[(s_ve, s_ve.n)], inc=(s_ve, 1))
            rb = r_.unsqueeze(2).broadcast_to([128, nb, 16])
            P.emit("vector", lambda e: e.tensor_tensor(
                u.rearrange("p (b w) -> p b w", w=16), src3, rb, op=ALU.mult),
                waits=[(s_ve, s_ve.n)], inc=(s_ve, 1))
            P.emit("vector", lambda e: e.tensor_scalar(
                M.bitcast(U32), u.bitcast(U32), 0x7F800000, None, op0=ALU.bitwise_and),
                waits=[(s_ve, s_ve.n)], inc=(s_ve, 1))
            P.emit("vector", lambda e: e.tensor_scalar(
                M, M, MAGIC, MAGIC, op0=ALU.mult, op1=ALU.max),
                waits=[(s_ve, s_ve.n)], inc=(s_ve, 1))
            ve_m = s_ve.n
            P.emit("gpsimd", lambda e: e.tensor_tensor(u, u, M, op=ALU.add),
                   waits=[(s_ve, ve_m), (s_po, s_po.n)], inc=(s_po, 1))
            P.emit("gpsimd", lambda e: e.tensor_tensor(u, u, M, op=ALU.subtract),
                   waits=[(s_po, s_po.n)], inc=(s_po, 1))
            return (s_po.n, u_off, qs_off)

        def quant_deq(tok, out_bf, F, extra_pe_wait=0):
            po_q, u_off, qs_off = tok
            nb = F // 16
            u = vf(u_off, F)
            sc_ = vf(qs_off + 4 * nb, nb)
            scb = sc_.unsqueeze(2).broadcast_to([128, nb, 16])
            P.emit("vector", lambda e: e.tensor_tensor(
                out_bf.rearrange("p (b w) -> p b w", w=16),
                u.rearrange("p (b w) -> p b w", w=16), scb, op=ALU.mult),
                waits=[(s_po, po_q)] + ([(s_pe, extra_pe_wait)] if extra_pe_wait else []),
                inc=(s_ve, 1))
            return s_ve.n

        def quant(src_f32, out_bf, u_off, m_off, qs_off, F, extra_ve_wait=0):
            tok = quant_pre(src_f32, u_off, m_off, qs_off, F)
            return quant_deq(tok, out_bf, F, extra_pe_wait=extra_ve_wait)

        def bank_chunks(a, b):
            cuts = [a]
            nxt = (a // 512 + 1) * 512
            while nxt < b:
                cuts.append(nxt)
                nxt += 512
            cuts.append(b)
            return list(zip(cuts[:-1], cuts[1:]))

        wk_i = 0
        stg_rd = [0, 0]
        stgb_rd = [0, 0]
        st2_rd = [0]
        for win, wout in ((wqt_in, wq_d), (wkt_in, wk_d), (wvt_in, wv_d), (wot_in, wo_d)):
            for c in range(NCD):
                wp = wk_i % 2
                wk_i += 1
                stg, stgb = vf(W_STG[wp], D), vb(W_QBF[wp], D)
                P.emit("sync", lambda e, win=win, c=c, stg=stg: e.dma_start(
                    out=stg, in_=win[c * 128:(c + 1) * 128, :]),
                    waits=[(s_w, s_w.n), (s_ve, stg_rd[wp])], inc=(s_w, 16))
                P.emit("vector", lambda e, stg=stg, stgb=stgb: e.tensor_copy(stgb, stg),
                       waits=[(s_w, s_w.n), (s_ws, s_ws.n)], inc=(s_ve, 1))
                stg_rd[wp] = s_ve.n
                P.emit("sync", lambda e, wout=wout, c=c, stgb=stgb: e.dma_start(
                    out=wout[:, c, :], in_=stgb),
                    waits=[(s_ve, s_ve.n)], inc=(s_ws, 16))
        ws_attnW = s_ws.n
        P.wait("vector", [(s_ws, ws_attnW)])

        wq_items = []
        for win, wout in ((wg_in, wg_d), (wu_in, wu_d)):
            for ic in range(NI):
                wq_items.append((win, wout, "gu", ic, 0, D))
        for c in range(NCD):
            for p0 in range(0, I, 1024):
                wq_items.append((wd_in, wd_d, "wd", c, p0, min(1024, I - p0)))
        NWQ = len(wq_items)
        stg_free = [stg_rd[0], stg_rd[1], 0]
        load_tok = [0] * NWQ
        store_tok = [0] * NWQ
        deq_pend = [None] * NWQ

        def wq_load(j):
            win, wout, kind, c, p0, plen = wq_items[j]
            sl = j % 3
            sem = s_wl[sl]
            stg = vf(W_STG[sl], plen)
            P.emit("sync" if j % 2 == 0 else "scalar",
                   lambda e, win=win, c=c, p0=p0, plen=plen, stg=stg: e.dma_start(
                       out=stg, in_=win[c * 128:(c + 1) * 128, p0:p0 + plen]),
                   waits=[(sem, sem.n), (s_ve, stg_free[sl])], inc=(sem, 16))
            load_tok[j] = sem.n

        def wq_flush(k):
            win, wout, kind, c, p0, plen, tok = deq_pend[k]
            wp = k % 2
            g = plen // 128
            stgb = vb(W_QBF[wp], plen)
            ve_q = quant_deq(tok, stgb, plen, extra_pe_wait=stgb_rd[wp])
            for t in range(g):
                P.emit("tensor", lambda e, stgb=stgb, t=t: e.transpose(
                    ps_t[:, t * 128:(t + 1) * 128], stgb[:, t * 128:(t + 1) * 128], idn[:]),
                    waits=([(s_ve, ve_q), (s_ac, s_ac.n), (s_po, po_const)] if t == 0 else []),
                    inc=((s_pe, 1) if t == g - 1 else None))
            stgb_rd[wp] = s_pe.n
            stg2 = vb(W_ST2[wp], plen)
            ssem = s_ws2[wp]
            P.emit("scalar", lambda e, stg2=stg2, g=g: e.copy(stg2, ps_t[:, 0:g * 128]),
                   waits=[(s_pe, s_pe.n)] + ([(ssem, store_tok[k - 2])] if k >= 2 else []),
                   inc=(s_ac, 1))
            ac_e = s_ac.n
            if kind == "gu":
                P.emit("sync" if k % 2 == 0 else "scalar",
                       lambda e, wout=wout, c=c, stg2=stg2: e.dma_start(
                           out=wout[:, :, c * 128:(c + 1) * 128],
                           in_=stg2.rearrange("p (c i) -> p c i", i=128)),
                       waits=[(s_ac, ac_e), (ssem, ssem.n)], inc=(ssem, 16))
            else:
                t0 = p0 // 128
                P.emit("sync" if k % 2 == 0 else "scalar",
                       lambda e, t0=t0, c=c, g=g, stg2=stg2: e.dma_start(
                           out=wd_d[:, t0:t0 + g, c * 128:(c + 1) * 128],
                           in_=stg2.rearrange("p (t i) -> p t i", i=128)),
                       waits=[(s_ac, ac_e), (ssem, ssem.n)], inc=(ssem, 16))
            store_tok[k] = ssem.n

        for j in range(min(3, NWQ)):
            wq_load(j)
        for k in range(NWQ):
            win, wout, kind, c, p0, plen = wq_items[k]
            sl = k % 3
            P.wait("vector", [(s_wl[sl], load_tok[k])])
            ve0 = s_ve.n
            tok = quant_pre(vf(W_STG[sl], plen), W_U[k % 2], W_M[k % 2], W_QS[k % 2], plen)
            stg_free[sl] = ve0 + 4
            deq_pend[k] = (win, wout, kind, c, p0, plen, tok)
            if k + 3 < NWQ:
                wq_load(k + 3)
            if k >= 1:
                wq_flush(k - 1)
        wq_flush(NWQ - 1)
        ws_stageW2 = [s_ws2[0].n, s_ws2[1].n]

        for b in range(NB):
            barrier()

            wa_sb = vb(WA, NCD * D).rearrange("p (c n) -> p c n", n=D)
            wv_sb = vb(WVT, NCD * D).rearrange("p (c n) -> p c n", n=D)
            P.emit("sync", lambda e, wv_sb=wv_sb: e.dma_start(out=wv_sb, in_=wv_d[:]),
                   waits=[(s_ws, ws_attnW)], inc=(s_w, 16))
            P.emit("sync", lambda e, wa_sb=wa_sb: e.dma_start(out=wa_sb, in_=wq_d[:]),
                   inc=(s_w, 16))
            w_attn_ld = s_w.n

            xnt = vb(XNT, NCD * S).rearrange("p (c q) -> p c q", q=S)
            xnqt = vb(XNQT, NCD * S).rearrange("p (c q) -> p c q", q=S)
            vst = vb(VST, NT * 16 * 65).rearrange("p (t h n) -> p t h n", h=16, n=65)
            P.emit("vector", lambda e, vst=vst: e.memset(
                vst[:, :, :, 64].rearrange("p t h -> p (t h)"), 1.0), inc=(s_ve, 1))

            ac_vevac = 0
            ac_xnqt = []
            x_rd = [(0, 0), (0, 0)]
            xn_rd = [0, 0]
            st_rd = [0, 0]
            xnb_rd = 0
            xnq_rd = [0, 0]
            sqrt_ac = [0, 0]
            stats_t = {}

            def ln_stage1(t):
                pp = t % 2
                xt = vf(L_X[pp], D)
                row0 = b * S + t * 128
                P.emit("sync", lambda e, xt=xt, row0=row0: e.dma_start(
                    out=xt, in_=x_in[row0:row0 + 128, :]),
                    waits=[(s_x, s_x.n), (s_ac, x_rd[pp][0]), (s_ve, x_rd[pp][1])],
                    inc=(s_x, 16))
                x_ld = s_x.n
                sq = vf(L_XN[pp], D)
                P.emit("scalar", lambda e, xt=xt, sq=sq: e.activation(sq, xt, AF.Square),
                       waits=[(s_x, x_ld), (s_ve, xn_rd[pp])], inc=(s_ac, 1))
                ac_sq = s_ac.n
                stb = STATS[pp]
                s1, s2 = vf(stb, 1), vf(stb + 8, 1)
                mean, var = vf(stb + 16, 1), vf(stb + 24, 1)
                sd, rstd, nmb, m2 = (vf(stb + 32, 1), vf(stb + 40, 1),
                                     vf(stb + 48, 1), vf(stb + 56, 1))
                P.emit("vector", lambda e, xt=xt, s1=s1: e.reduce_sum(s1, xt, axis=AX.X),
                       waits=[(s_x, x_ld), (s_ve, s_ve.n), (s_ac, st_rd[pp])], inc=(s_ve, 1))
                P.emit("vector", lambda e, sq=sq, s2=s2: e.reduce_sum(s2, sq, axis=AX.X),
                       waits=[(s_ac, ac_sq), (s_ve, s_ve.n)], inc=(s_ve, 1))
                P.emit("vector", lambda e, s1=s1, mean=mean: e.tensor_scalar(
                    mean, s1, 1.0 / D, None, op0=ALU.mult),
                    waits=[(s_ve, s_ve.n)], inc=(s_ve, 1))
                P.emit("vector", lambda e, s2=s2, var=var: e.tensor_scalar(
                    var, s2, 1.0 / D, None, op0=ALU.mult),
                    waits=[(s_ve, s_ve.n)], inc=(s_ve, 1))
                P.emit("vector", lambda e, m2=m2, mean=mean: e.tensor_scalar(
                    m2, mean, mean, None, op0=ALU.mult),
                    waits=[(s_ve, s_ve.n)], inc=(s_ve, 1))
                P.emit("vector", lambda e, var=var, m2=m2: e.tensor_tensor(
                    var, var, m2, op=ALU.subtract),
                    waits=[(s_ve, s_ve.n)], inc=(s_ve, 1))
                ve_stats = s_ve.n
                P.emit("scalar", lambda e, sd=sd, var=var: e.activation(
                    sd, var, AF.Sqrt, bias=ceps[:]),
                    waits=[(s_ve, ve_stats), (s_po, po_const)], inc=(s_ac, 1))
                sqrt_ac[pp] = s_ac.n
                stats_t[t] = (xt, x_ld, sd, rstd, nmb, mean)

            ln_stage1(0)
            for t in range(NT):
                pp = t % 2
                xt, x_ld, sd, rstd, nmb, mean = stats_t.pop(t)
                P.emit("vector", lambda e, rstd=rstd, sd=sd: e.reciprocal(rstd, sd),
                       waits=[(s_ac, sqrt_ac[pp]), (s_ve, s_ve.n)], inc=(s_ve, 1))
                P.emit("vector", lambda e, nmb=nmb, mean=mean, rstd=rstd: e.tensor_scalar(
                    nmb, mean, rstd, -1.0, op0=ALU.mult, op1=ALU.mult),
                    waits=[(s_ve, s_ve.n)], inc=(s_ve, 1))
                ve_st = s_ve.n
                xn = vf(L_XN[pp], D)
                P.emit("scalar", lambda e, xn=xn, xt=xt, rstd=rstd, nmb=nmb: e.activation(
                    xn, xt, AF.Identity, bias=nmb, scale=rstd),
                    waits=[(s_ve, ve_st)], inc=(s_ac, 1))
                ac_xn = s_ac.n
                xnb = vb(L_XNB, D)
                P.emit("scalar", lambda e, xnb=xnb, xt=xt, rstd=rstd, nmb=nmb: e.activation(
                    xnb, xt, AF.Identity, bias=nmb, scale=rstd),
                    waits=[(s_pe, xnb_rd)], inc=(s_ac, 1))
                ac_xnb = s_ac.n
                st_rd[pp] = ac_xnb
                x_rd[pp] = (ac_xnb, ve_st)
                if t + 1 < NT:
                    ln_stage1(t + 1)
                xnq = vb(L_XNQ[pp], D)
                P.wait("vector", [(s_ac, ac_xn), (s_pe, xnq_rd[pp])])
                tok0 = quant_pre(xn[:, 0:512], L_U[0], L_M[0], L_QS[0], 512)
                tok1 = quant_pre(xn[:, 512:1024], L_U[1], L_M[1], L_QS[1], 512)
                quant_deq(tok0, xnq[:, 0:512], 512)
                ve_xnq = quant_deq(tok1, xnq[:, 512:1024], 512)
                xn_rd[pp] = ve_xnq
                if t >= 1:
                    pxnq, pt = pending_xnq
                    for c in range(NCD):
                        P.emit("tensor", lambda e, pxnq=pxnq, c=c: e.transpose(
                            ps_t[:, c * 128:(c + 1) * 128],
                            pxnq[:, c * 128:(c + 1) * 128], idn[:]),
                            waits=([(s_ve, pending_ve), (s_ac, s_ac.n)] if c == 0 else []),
                            inc=((s_pe, 1) if c == NCD - 1 else None))
                    xnq_rd[pt % 2] = s_pe.n
                    P.emit("scalar", lambda e, pt=pt, xnqt=xnqt: e.copy(
                        xnqt[:, :, pt * 128:(pt + 1) * 128],
                        ps_t[:].rearrange("p (c q) -> p c q", q=128)),
                        waits=[(s_pe, s_pe.n)], inc=(s_ac, 1))
                    ac_xnqt.append(s_ac.n)
                pending_xnq = (xnq, t)
                pending_ve = ve_xnq
                for c in range(NCD):
                    P.emit("tensor", lambda e, xnb=xnb, c=c: e.transpose(
                        ps_t[:, c * 128:(c + 1) * 128], xnb[:, c * 128:(c + 1) * 128], idn[:]),
                        waits=([(s_ac, s_ac.n), (s_po, po_const)] if c == 0 else []),
                        inc=((s_pe, 1) if c == NCD - 1 else None))
                xnb_rd = s_pe.n
                P.emit("scalar", lambda e, t=t, xnt=xnt: e.copy(
                    xnt[:, :, t * 128:(t + 1) * 128],
                    ps_t[:].rearrange("p (c q) -> p c q", q=128)),
                    waits=[(s_pe, s_pe.n)], inc=(s_ac, 1))
                ac_xnt_t = s_ac.n
                for c in range(NCD):
                    for ci, (ca, cb) in enumerate(bank_chunks(0, D)):
                        P.emit("tensor", lambda e, t=t, c=c, ca=ca, cb=cb, xnt=xnt,
                               wv_sb=wv_sb: e.matmul(
                            ps_b[:, ca:cb], xnt[:, c, t * 128:(t + 1) * 128],
                            wv_sb[:, c, ca:cb],
                            start=(c == 0), stop=(c == NCD - 1)),
                            waits=([(s_ac, ac_xnt_t), (s_w, w_attn_ld)]
                                   if (c == 0 and ci == 0) else []),
                            inc=((s_pe, 1) if (c == NCD - 1 and cb == D) else None))
                P.emit("scalar", lambda e, t=t, vst=vst: e.copy(
                    vst[:, t, :, 0:64], ps_b[:].rearrange("p (h v) -> p h v", v=64)),
                    waits=[(s_pe, s_pe.n)], inc=(s_ac, 1))
                ac_vevac = s_ac.n
            pxnq, pt = pending_xnq
            for c in range(NCD):
                P.emit("tensor", lambda e, pxnq=pxnq, c=c: e.transpose(
                    ps_t[:, c * 128:(c + 1) * 128],
                    pxnq[:, c * 128:(c + 1) * 128], idn[:]),
                    waits=([(s_ve, pending_ve), (s_ac, s_ac.n)] if c == 0 else []),
                    inc=((s_pe, 1) if c == NCD - 1 else None))
            xnq_rd[pt % 2] = s_pe.n
            P.emit("scalar", lambda e, pt=pt, xnqt=xnqt: e.copy(
                xnqt[:, :, pt * 128:(pt + 1) * 128],
                ps_t[:].rearrange("p (c q) -> p c q", q=128)),
                waits=[(s_pe, s_pe.n)], inc=(s_ac, 1))
            ac_xnqt.append(s_ac.n)

            qt_sb = vb(QT, NP * S).rearrange("p (r q) -> p r q", q=S)
            kt_sb = vb(KT, NP * S).rearrange("p (r q) -> p r q", q=S)
            evac_of_buf = {0: ac_vevac, 1: ac_vevac}
            gidx = 0
            w_ld = w_attn_ld
            for widx, dst in enumerate((qt_sb, kt_sb)):
                if widx == 1:
                    P.emit("sync", lambda e, wa_sb=wa_sb: e.dma_start(
                        out=wa_sb, in_=wk_d[:]),
                        waits=[(s_pe, s_pe.n)], inc=(s_w, 16))
                    w_ld = s_w.n
                for p in range(NP):
                    for qc in range(NQC):
                        bi = gidx % 2
                        ps = ps_a if bi == 0 else ps_b
                        gidx += 1
                        for c in range(NCD):
                            for ci, (ca, cb) in enumerate(bank_chunks(0, QC)):
                                P.emit("tensor", lambda e, ps=ps, c=c, p=p, qc=qc,
                                       ca=ca, cb=cb: e.matmul(
                                    ps[:, ca:cb], wa_sb[:, c, p * 128:(p + 1) * 128],
                                    xnt[:, c, qc * QC + ca:qc * QC + cb],
                                    start=(c == 0), stop=(c == NCD - 1)),
                                    waits=([(s_ac, max(evac_of_buf[bi], ac_xnqt[-1])),
                                            (s_w, w_ld)] if (c == 0 and ci == 0) else []),
                                    inc=((s_pe, 1) if (c == NCD - 1 and cb == QC) else None))
                        P.emit("scalar", lambda e, dst=dst, p=p, qc=qc, ps=ps: e.copy(
                            dst[:, p, qc * QC:(qc + 1) * QC], ps[:, 0:QC]),
                            waits=[(s_pe, s_pe.n)], inc=(s_ac, 1))
                        evac_of_buf[bi] = s_ac.n
            pe_qkv_done = s_pe.n

            atnt = vb(ATNT, NP * S).rearrange("p (r q) -> p r q", q=S)
            zbuf = vf(ZBUF, S // 8, parts=128)
            ws_zg2 = [0, 0]
            QCH = min(512, QC)
            NQCH = S // QCH
            scbuf = [ps_a[:, 0:512], ps_a[:, 512:1024], ps_b[:, 0:512], ps_b[:, 512:1024]]
            pvps = [ps_c[0:65, 0:512], ps_c[0:65, 512:1024]]
            pvreader4 = [evac_of_buf[0], evac_of_buf[0], evac_of_buf[1], evac_of_buf[1]]
            pc_rd = [(0, 0), (0, 0)]
            for pr in range(NP):
                for qc in range(NQCH):
                    jmax = (qc + 1) * QCH // 128
                    sc_pe = {}
                    for jj in range(0, jmax, 2):
                        js = [j for j in range(jj, min(jj + 2, jmax))]
                        for j in js:
                            q0 = max(qc * QCH, j * 128)
                            n = (qc + 1) * QCH - q0
                            for s_ in (0, 1):
                                ps = scbuf[(j % 2) * 2 + s_]
                                P.emit("tensor", lambda e, ps=ps, pr=pr, s_=s_, j=j,
                                       q0=q0, n=n, kt_sb=kt_sb, qt_sb=qt_sb: e.matmul(
                                    ps[:, 0:n],
                                    kt_sb[s_ * 64:(s_ + 1) * 64, pr, j * 128:(j + 1) * 128],
                                    qt_sb[s_ * 64:(s_ + 1) * 64, pr, q0:q0 + n],
                                    start=True, stop=True),
                                    waits=[(s_ac, pvreader4[(j % 2) * 2 + s_])],
                                    inc=(s_pe, 1))
                                sc_pe[(j, s_)] = s_pe.n
                        for j in js:
                            q0 = max(qc * QCH, j * 128)
                            n = (qc + 1) * QCH - q0
                            off = q0 - qc * QCH
                            for s_ in (0, 1):
                                ps = scbuf[(j % 2) * 2 + s_]
                                eb = vb(EXPB_BASE + 2 * QCH * ((j % 3) * 2 + s_), QCH)
                                P.emit("scalar", lambda e, eb=eb, ps=ps, n=n: e.activation(
                                    eb[:, 0:n], ps[:, 0:n], AF.Exp, scale=0.125),
                                    waits=[(s_pe, sc_pe[(j, s_)])], inc=(s_ac, 1))
                                ac_exp = s_ac.n
                                pvreader4[(j % 2) * 2 + s_] = ac_exp
                                po_mask = 0
                                if j * 128 >= qc * QCH:
                                    P.emit("gpsimd", lambda e, eb=eb: e.tensor_tensor(
                                        eb[:, 0:128], eb[:, 0:128], tri[:], op=ALU.mult),
                                        waits=[(s_ac, ac_exp), (s_po, s_po.n)], inc=(s_po, 1))
                                    po_mask = s_po.n
                                pv = pvps[s_]
                                P.emit("tensor", lambda e, j=j, eb=eb, off=off, n=n, pv=pv,
                                       pr=pr, s_=s_, vst=vst, jm=jmax: e.matmul(
                                    pv[:, off:off + n],
                                    vst[:, j, pr * 2 + s_, 0:65], eb[:, 0:n],
                                    start=(j == 0), stop=(j == jm - 1)),
                                    waits=[(s_ac, ac_exp)]
                                          + ([(s_po, po_mask)] if po_mask else [])
                                          + ([(s_ve, pc_rd[s_][0]),
                                              (s_ac, pc_rd[s_][1])] if j == 0 else []),
                                    inc=(s_pe, 1))
                    pe_pv = s_pe.n
                    for s_ in (0, 1):
                        h = pr * 2 + s_
                        pv = pvps[s_]
                        P.emit("vector", lambda e, pr=pr, s_=s_, qc=qc, atnt=atnt, pv=pv: e.tensor_copy(
                            atnt[s_ * 64:(s_ + 1) * 64, pr, qc * QCH:(qc + 1) * QCH],
                            pv[0:64, 0:QCH]),
                            waits=[(s_pe, pe_pv), (s_ve, s_ve.n)], inc=(s_ve, 1))
                        ve_pc = s_ve.n
                        ztmp = vf(RZBC if s_ == 0 else RZBC2, S, parts=1)
                        P.emit("scalar", lambda e, qc=qc, ztmp=ztmp, pv=pv: e.copy(
                            ztmp[0:1, qc * QCH:(qc + 1) * QCH], pv[64:65, 0:QCH]),
                            waits=[(s_pe, pe_pv)]
                                  + ([(s_zg[s_], ws_zg2[s_])] if (qc == 0 and ws_zg2[s_]) else []),
                            inc=(s_ac, 1))
                        pc_rd[s_] = (ve_pc, s_ac.n)
                for s_ in (0, 1):
                    h = pr * 2 + s_
                    ztmp = vf(RZBC if s_ == 0 else RZBC2, S, parts=1)
                    P.emit("scalar", lambda e, h=h, ztmp=ztmp: e.dma_start(
                        out=rz_d[h:h + 1, :], in_=ztmp[0:1, :]),
                        waits=[(s_ac, s_ac.n)], inc=(s_zg[s_], 16))
                    ws_zg2[s_] = s_zg[s_].n
            pe_attn_done = s_pe.n

            wo_sb = vb(WOT, NCD * D).rearrange("p (c n) -> p c n", n=D)
            P.emit("sync", lambda e, wo_sb=wo_sb: e.dma_start(out=wo_sb, in_=wo_d[:]),
                   waits=[(s_pe, pe_attn_done)], inc=(s_w, 16))
            w_wo_ld = s_w.n
            wgq = vb(F_WG, NCD * I).rearrange("p (c i) -> p c i", i=I)
            wuq = vb(F_WU, NCD * I).rearrange("p (c i) -> p c i", i=I)
            P.emit("sync", lambda e, wgq=wgq: e.dma_start(out=wgq, in_=wg_d[:]),
                   waits=[(s_ws2[0], ws_stageW2[0]), (s_ws2[1], ws_stageW2[1]),
                          (s_w, s_w.n)], inc=(s_w, 16))
            P.emit("sync", lambda e, wuq=wuq: e.dma_start(out=wuq, in_=wu_d[:]),
                   inc=(s_w, 16))
            w_gu_ld = s_w.n

            P.emit("scalar", lambda e, zbuf=zbuf: e.dma_start(
                out=zbuf[:, :], in_=rz_d.rearrange("h (a c) -> (h a) c", a=8)),
                waits=[(s_zg[0], ws_zg2[0]), (s_zg[1], ws_zg2[1])], inc=(s_ws, 16))
            ws_zld = s_ws.n
            P.emit("vector", lambda e, zbuf=zbuf: e.reciprocal(zbuf[:, :], zbuf[:, :]),
                   waits=[(s_ws, ws_zld), (s_ve, s_ve.n)], inc=(s_ve, 1))
            ve_rz = s_ve.n
            P.emit("scalar", lambda e, zbuf=zbuf: e.dma_start(
                out=rz_d.rearrange("h (a c) -> (h a) c", a=8), in_=zbuf[:, :]),
                   waits=[(s_ve, ve_rz)], inc=(s_ws, 16))
            ws_rz = s_ws.n
            ve_norm_p = []
            sc_hist = [0, 0]
            sel2 = vf(EXPB_BASE, 128, parts=2)
            P.emit("gpsimd", lambda e, sel2=sel2: e.memset(sel2[:], 1.0),
                   waits=[(s_pe, pe_attn_done), (s_po, s_po.n)], inc=(s_po, 1))
            P.emit("gpsimd", lambda e, sel2=sel2: e.affine_select(
                out=sel2[:], in_=sel2[:], pattern=[[1, 128]], compare_op=ALU.is_ge,
                fill=0.0, base=0, channel_multiplier=-64),
                waits=[(s_po, s_po.n)], inc=(s_po, 1))
            P.emit("gpsimd", lambda e, sel2=sel2: e.affine_select(
                out=sel2[:], in_=sel2[:], pattern=[[-1, 128]], compare_op=ALU.is_ge,
                fill=0.0, base=63, channel_multiplier=64),
                waits=[(s_po, s_po.n)], inc=(s_po, 1))
            po_sel = s_po.n
            ve_norm_last = 0
            for p in range(NP):
                zpair = vf(RZBC if p % 2 == 0 else RZBC2, S, parts=2)
                ssem = s_sc if p % 2 == 0 else s_sc2
                P.emit("scalar", lambda e, p=p, zpair=zpair: e.dma_start(
                    out=zpair[0:2, :], in_=rz_d[2 * p:2 * p + 2, :]),
                    waits=[(s_ws, ws_rz)]
                          + ([(s_ve, sc_hist[p % 2])] if sc_hist[p % 2] else []),
                    inc=(ssem, 16))
                sc_rz = (ssem, ssem.n)
                HWID = min(1024, S)
                CWID = min(512, S)
                for half in range(S // HWID):
                    nchunk = HWID // CWID
                    for ci in range(nchunk):
                        P.emit("tensor", lambda e, zpair=zpair, half=half, ci=ci,
                               HWID=HWID, CWID=CWID: e.matmul(
                            ps_c[:, ci * CWID:(ci + 1) * CWID], sel2[0:2, :],
                            zpair[0:2, half * HWID + ci * CWID: half * HWID + (ci + 1) * CWID],
                            start=True, stop=True),
                            waits=([sc_rz, (s_po, po_sel)]
                                   + ([(s_ve, s_ve.n), (s_ac, s_ac.n)]
                                      if (p == 0 and half == 0) else
                                      [(s_ve, ve_norm_last)])
                                   if ci == 0 else []),
                            inc=((s_pe, 1) if ci == nchunk - 1 else None))
                    pe_bc = s_pe.n
                    P.emit("vector", lambda e, p=p, half=half, atnt=atnt, HWID=HWID: e.tensor_tensor(
                        atnt[:, p, half * HWID:(half + 1) * HWID],
                        atnt[:, p, half * HWID:(half + 1) * HWID],
                        ps_c[:, 0:HWID], op=ALU.mult),
                        waits=[(s_pe, pe_bc), (s_ve, s_ve.n)], inc=(s_ve, 1))
                    ve_norm_last = s_ve.n
                sc_hist[p % 2] = s_ve.n
                ve_norm_p.append(s_ve.n)
            ve_norm = s_ve.n

            ws_ot_last = 0
            evac_a = max(pvreader4[0], pvreader4[1])
            for t in range(NT):
                for p in range(NP):
                    for ci, (ca, cb) in enumerate(bank_chunks(0, D)):
                        w0 = []
                        if ci == 0:
                            if t == 0:
                                w0 = [(s_ve, ve_norm_p[p]), (s_w, w_wo_ld)]
                                if p == 0:
                                    w0.append((s_ac, evac_a))
                            elif p == 0:
                                w0 = [(s_ac, evac_a)]
                        P.emit("tensor", lambda e, t=t, p=p, ca=ca, cb=cb, atnt=atnt,
                               wo_sb=wo_sb: e.matmul(
                            ps_a[:, ca:cb], atnt[:, p, t * 128:(t + 1) * 128],
                            wo_sb[:, p, ca:cb],
                            start=(p == 0), stop=(p == NP - 1)),
                            waits=w0,
                            inc=((s_pe, 1) if (p == NP - 1 and cb == D) else None))
                pe_op = s_pe.n
                ot = vf(OT0 if t % 2 == 0 else OT1, D)
                P.emit("scalar", lambda e, ot=ot: e.copy(ot, ps_a[:]),
                       waits=[(s_pe, pe_op)] + ([(s_ws, ws_ot_last)] if (t >= 2 and ws_ot_last) else []),
                       inc=(s_ac, 1))
                evac_a = s_ac.n
                P.emit("sync", lambda e, ot=ot, b=b, t=t: e.dma_start(
                    out=attn_scr[b * S + t * 128: b * S + (t + 1) * 128, :], in_=ot),
                    waits=[(s_ac, s_ac.n)], inc=(s_ws, 16))
                ws_ot_last = s_ws.n
            pe_outproj_done = s_pe.n
            ws_attn = s_ws.n

            barrier()
            wdq = vb(F_WD, NI * D).rearrange("p (t n) -> p t n", n=D)
            P.emit("sync", lambda e, wdq=wdq: e.dma_start(out=wdq, in_=wd_d[:]),
                   waits=[(s_ws2[0], ws_stageW2[0]), (s_ws2[1], ws_stageW2[1]),
                          (s_pe, pe_outproj_done)], inc=(s_sc2, 16))
            w_d_ld = s_sc2.n


            NICH = (I + ICH - 1) // ICH
            hbuf = vf(F_H, I)
            hq = vb(F_HQ, I)
            hqt = vb(F_HQT, NI * 128)
            silu_ac = [0, 0]
            hmul_ve = [0, 0]
            hbuf_rd = [0] * NICH
            st_xt = [0, 0]
            po_f_prev = 0
            tr_pe_done = 0
            ac_tr_all = 0
            ve_f_prev = 0
            prev = None
            for t in range(NT + 1):
                pp = t % 2
                if t < NT:
                    row0 = b * S + t * 128
                    xt = vf(F_XT[pp], D)
                    P.emit("sync", lambda e, xt=xt, row0=row0: e.dma_start(
                        out=xt, in_=x_in[row0:row0 + 128, :]),
                        waits=[(s_x, s_x.n)] + ([(s_st, st_xt[pp])] if st_xt[pp] else []),
                        inc=(s_x, 16))
                    x_ld = s_x.n
                    pend_deq = None
                    for ici in range(NICH):
                        ip = ici % 2
                        psg = ps_a if ip == 0 else ps_b
                        i0 = ici * ICH
                        ilen = min(ICH, I - i0)
                        gw = [(s_ac, silu_ac[ip])] + ([(s_w, w_gu_ld)] if (t == 0 and ici == 0) else [])
                        for c in range(NCD):
                            P.emit("tensor", lambda e, t=t, c=c, i0=i0, ilen=ilen, psg=psg,
                                   xnqt=xnqt, wgq=wgq: e.matmul(
                                psg[:, 0:ilen], xnqt[:, c, t * 128:(t + 1) * 128],
                                wgq[:, c, i0:i0 + ilen], start=(c == 0), stop=(c == NCD - 1)),
                                waits=(gw if c == 0 else []),
                                inc=((s_pe, 1) if c == NCD - 1 else None))
                        pe_g = s_pe.n
                        for c in range(NCD):
                            P.emit("tensor", lambda e, t=t, c=c, i0=i0, ilen=ilen, psg=psg,
                                   xnqt=xnqt, wuq=wuq: e.matmul(
                                psg[:, 512:512 + ilen], xnqt[:, c, t * 128:(t + 1) * 128],
                                wuq[:, c, i0:i0 + ilen], start=(c == 0), stop=(c == NCD - 1)),
                                waits=([(s_ve, hmul_ve[ip])] if c == 0 else []),
                                inc=((s_pe, 1) if c == NCD - 1 else None))
                        pe_u = s_pe.n
                        P.emit("scalar", lambda e, i0=i0, ilen=ilen, psg=psg, hbuf=hbuf: e.activation(
                            hbuf[:, i0:i0 + ilen], psg[:, 0:ilen], AF.Silu),
                            waits=[(s_pe, pe_u)] + ([(s_ve, hbuf_rd[ici])] if hbuf_rd[ici] else []),
                            inc=(s_ac, 1))
                        ac_si = s_ac.n
                        silu_ac[ip] = ac_si
                        P.emit("vector", lambda e, i0=i0, ilen=ilen, psg=psg, hbuf=hbuf: e.tensor_tensor(
                            hbuf[:, i0:i0 + ilen], hbuf[:, i0:i0 + ilen],
                            psg[:, 512:512 + ilen], op=ALU.mult),
                            waits=[(s_ac, ac_si), (s_pe, pe_u), (s_ve, s_ve.n)], inc=(s_ve, 1))
                        hmul_ve[ip] = s_ve.n
                        tok = quant_pre(hbuf[:, i0:i0 + ilen], F_U[ip], F_M[ip],
                                        F_QS[ip], ilen)
                        if pend_deq is not None:
                            pi0, pilen, ptok = pend_deq
                            hbuf_rd[ici - 1] = quant_deq(
                                ptok, hq[:, pi0:pi0 + pilen], pilen,
                                extra_pe_wait=tr_pe_done)
                        pend_deq = (i0, ilen, tok)
                    pi0, pilen, ptok = pend_deq
                    hbuf_rd[NICH - 1] = quant_deq(
                        ptok, hq[:, pi0:pi0 + pilen], pilen, extra_pe_wait=tr_pe_done)
                    ve_hq = s_ve.n
                if prev is not None:
                    pxt, px_ld, prow0 = prev
                    pat = vf(F_AT, D)
                    P.emit("sync", lambda e, pat=pat, prow0=prow0: e.dma_start(
                        out=pat, in_=attn_scr[prow0:prow0 + 128, :]),
                        waits=[(s_sc, s_sc.n)]
                              + ([(s_ws, ws_attn)] if t == 1 else
                                 ([(s_po, po_f_prev)] if po_f_prev else [])),
                        inc=(s_sc, 16))
                    pat_ld = s_sc.n
                    for it in range(NI):
                        for ci, (ca, cb) in enumerate(bank_chunks(0, D)):
                            P.emit("tensor", lambda e, it=it, ca=ca, cb=cb, hqt=hqt,
                                   wdq=wdq: e.matmul(
                                ps_c[:, ca:cb], hqt[:, it * 128:(it + 1) * 128],
                                wdq[:, it, ca:cb],
                                start=(it == 0), stop=(it == NI - 1)),
                                waits=([(s_ac, ac_tr_all), (s_sc2, w_d_ld)]
                                       + ([(s_ve, ve_f_prev)] if ve_f_prev else [])
                                       if (it == 0 and ci == 0) else []),
                                inc=((s_pe, 1) if (it == NI - 1 and cb == D) else None))
                    pe_dn = s_pe.n
                    P.emit("vector", lambda e, pat=pat: e.tensor_tensor(
                        pat, ps_c[:], pat, op=ALU.add),
                        waits=[(s_pe, pe_dn), (s_sc, pat_ld), (s_ve, s_ve.n)], inc=(s_ve, 1))
                    ve_f = s_ve.n
                    ve_f_prev = ve_f
                    P.emit("gpsimd", lambda e, pat=pat, pxt=pxt: e.tensor_tensor(
                        pxt, pat, pxt, op=ALU.add),
                        waits=[(s_ve, ve_f), (s_x, px_ld), (s_po, s_po.n)], inc=(s_po, 1))
                    po_f = s_po.n
                    po_f_prev = po_f
                    P.emit("sync", lambda e, pxt=pxt, prow0=prow0: e.dma_start(
                        out=out_hbm[prow0:prow0 + 128, :], in_=pxt),
                        waits=[(s_po, po_f)], inc=(s_st, 16))
                    st_xt[(t - 1) % 2] = s_st.n
                if t < NT:
                    done = 0
                    while done < NI:
                        g = min(8, NI - done)
                        for k in range(g):
                            it = done + k
                            P.emit("tensor", lambda e, it=it, k=k, hq=hq: e.transpose(
                                ps_t[:, k * 128:(k + 1) * 128],
                                hq[:, it * 128:(it + 1) * 128], idn[:]),
                                waits=([(s_ve, ve_hq), (s_ac, s_ac.n)] if k == 0 else []),
                                inc=((s_pe, 1) if k == g - 1 else None))
                        P.emit("scalar", lambda e, done=done, g=g, hqt=hqt: e.copy(
                            hqt[:, done * 128:(done + g) * 128], ps_t[:, 0:g * 128]),
                            waits=[(s_pe, s_pe.n)], inc=(s_ac, 1))
                        done += g
                    ac_tr_all = s_ac.n
                    tr_pe_done = s_pe.n
                    prev = (xt, x_ld, row0)

        P.wait("sync", [(s_st, s_st.n)])

        with nc.Block() as block:
            @block.sync
            def _(e):
                for f in P.ops["sync"]:
                    f(e)

            @block.vector
            def _(e):
                for f in P.ops["vector"]:
                    f(e)

            @block.scalar
            def _(e):
                for f in P.ops["scalar"]:
                    f(e)

            @block.tensor
            def _(e):
                for f in P.ops["tensor"]:
                    f(e)

            @block.gpsimd
            def _(e):
                for f in P.ops["gpsimd"]:
                    f(e)

    return nc


def kernel(L0_attn_x, L0_attn_WQ, L0_attn_WK, L0_attn_WV, L0_attn_WO,
           L0_ffn_w_gate, L0_ffn_w_up, L0_ffn_w_down):
    x = np.ascontiguousarray(np.asarray(L0_attn_x, dtype=np.float32))
    B, S, D = x.shape
    NB = B // N_CORES
    wqt = np.ascontiguousarray(np.asarray(L0_attn_WQ, np.float32).transpose(2, 0, 1).reshape(D, D))
    wkt = np.ascontiguousarray(np.asarray(L0_attn_WK, np.float32).transpose(2, 0, 1).reshape(D, D))
    wvt = np.ascontiguousarray(np.asarray(L0_attn_WV, np.float32).transpose(2, 0, 1).reshape(D, D))
    wot = np.ascontiguousarray(np.asarray(L0_attn_WO, np.float32).transpose(0, 2, 1).reshape(D, D))
    wg = np.ascontiguousarray(np.asarray(L0_ffn_w_gate, np.float32))
    wu = np.ascontiguousarray(np.asarray(L0_ffn_w_up, np.float32))
    wd = np.ascontiguousarray(np.asarray(L0_ffn_w_down, np.float32))

    nc = bass.Bass("TRN2", target_bir_lowering=False, debug=False)
    build_core_program(nc, S=S, NB=NB)

    in_maps = []
    for core in range(N_CORES):
        shard = x[core * NB:(core + 1) * NB].reshape(NB * S, D)
        in_maps.append({
            "x": shard, "wqt": wqt, "wkt": wkt, "wvt": wvt, "wot": wot,
            "w_gate": wg, "w_up": wu, "w_down": wd,
        })
    res = run_bass_kernel_spmd(nc, in_maps, core_ids=list(range(N_CORES)))
    out = np.concatenate([r["out"].reshape(NB, S, D) for r in res.results], axis=0)
    return out.astype(np.float32)


# version 22
# speedup vs baseline: 1.1932x; 1.1932x over previous
import numpy as np
import concourse.bass as bass
import concourse.mybir as mybir
from concourse.bass_utils import run_bass_kernel_spmd
from contextlib import ExitStack

F32 = mybir.dt.float32
BF16 = mybir.dt.bfloat16
U32 = mybir.dt.uint32
ALU = mybir.AluOpType
AF = mybir.ActivationFunctionType
AX = mybir.AxisListType

MAGIC = float(np.float32(0.75 * 2 ** 23))
SIXTH = float(np.float32(1.0 / 6.0))
N_CORES = 8


class Sem:
    def __init__(self, es, nc, name):
        self.h = es.enter_context(nc.semaphore(name))
        self.n = 0


class Prog:
    def __init__(self):
        self.ops = {k: [] for k in ("sync", "vector", "scalar", "tensor", "gpsimd")}

    def emit(self, eng, fn, waits=(), inc=None):
        ws = [(s.h, int(v)) for (s, v) in waits if v > 0]
        k_inc = None
        if inc is not None:
            s, k = inc
            s.n += k
            k_inc = (s.h, k)

        def run(e, fn=fn, ws=ws, k_inc=k_inc):
            for h, v in ws:
                e.wait_ge(h, v)
            inst = fn(e)
            if k_inc is not None:
                inst.then_inc(*k_inc)

        self.ops[eng].append(run)

    def wait(self, eng, waits):
        ws = [(s.h, int(v)) for (s, v) in waits if v > 0]
        if not ws:
            return

        def run(e, ws=ws):
            for h, v in ws:
                e.wait_ge(h, v)

        self.ops[eng].append(run)


def build_core_program(nc, S=2048, NB=4, QC=None):
    D, DH, H, NP = 1024, 64, 16, 8
    NCD = D // 128
    I = 2816
    NI = I // 128
    ICH = 512
    NT = S // 128
    if QC is None:
        QC = min(1024, S)
    NQC = S // QC

    x_in = nc.dram_tensor("x", [NB * S, D], F32, kind="ExternalInput").ap()
    wqt_in = nc.dram_tensor("wqt", [D, D], F32, kind="ExternalInput").ap()
    wkt_in = nc.dram_tensor("wkt", [D, D], F32, kind="ExternalInput").ap()
    wvt_in = nc.dram_tensor("wvt", [D, D], F32, kind="ExternalInput").ap()
    wot_in = nc.dram_tensor("wot", [D, D], F32, kind="ExternalInput").ap()
    wg_in = nc.dram_tensor("w_gate", [I, D], F32, kind="ExternalInput").ap()
    wu_in = nc.dram_tensor("w_up", [I, D], F32, kind="ExternalInput").ap()
    wd_in = nc.dram_tensor("w_down", [D, I], F32, kind="ExternalInput").ap()
    out_hbm = nc.dram_tensor("out", [NB * S, D], F32, kind="ExternalOutput").ap()

    attn_scr = nc.dram_tensor("attn_scr", [NB * S, D], F32, kind="Internal").ap()
    rz_d = nc.dram_tensor("rz_d", [H, S], F32, kind="Internal").ap()
    wq_d = nc.dram_tensor("wq_d", [128, NCD, D], BF16, kind="Internal").ap()
    wk_d = nc.dram_tensor("wk_d", [128, NCD, D], BF16, kind="Internal").ap()
    wv_d = nc.dram_tensor("wv_d", [128, NCD, D], BF16, kind="Internal").ap()
    wo_d = nc.dram_tensor("wo_d", [128, NCD, D], BF16, kind="Internal").ap()
    wg_d = nc.dram_tensor("wg_d", [128, NCD, I], BF16, kind="Internal").ap()
    wu_d = nc.dram_tensor("wu_d", [128, NCD, I], BF16, kind="Internal").ap()
    wd_d = nc.dram_tensor("wd_d", [128, NI, D], BF16, kind="Internal").ap()

    es = ExitStack()
    with es:
        HEAP_BYTES = 211968
        heap = es.enter_context(nc.sbuf_tensor("heap", [128, HEAP_BYTES // 4], F32))
        idn = es.enter_context(nc.sbuf_tensor("idn", [128, 128], BF16))
        tri = es.enter_context(nc.sbuf_tensor("tri", [128, 128], BF16))
        ceps = es.enter_context(nc.sbuf_tensor("ceps", [128, 1], F32))

        def vf(off, cols, parts=128):
            assert off % 4 == 0 and off + 4 * cols <= HEAP_BYTES, (off, cols)
            return heap[0:parts, off // 4: off // 4 + cols]

        def vb(off, cols):
            assert off % 4 == 0 and off + 2 * cols <= HEAP_BYTES, (off, cols)
            n4 = (cols + 1) // 2
            return heap[:, off // 4: off // 4 + n4].bitcast(BF16)[:, 0:cols]

        SZ_T = 2 * NCD * S
        XNT = 0
        XNQT = SZ_T
        WVT = 2 * SZ_T
        WA = WVT + 2 * NCD * D
        ATNT = WVT
        QT = WA + 2 * NCD * D
        KT = QT + 2 * NP * S
        VST = KT + 2 * NP * S
        VST_END = VST + 2 * NT * 16 * 65

        F_WD = WVT
        F_H = F_WD + 2 * NI * D
        F_WG = F_H + 4 * I
        F_WU = F_WG + 2 * NCD * I
        F_END = F_WU + 2 * NCD * I
        assert F_END <= HEAP_BYTES, (F_END,)

        extra = [max(VST_END, F_END)]

        def alloc_extra(size):
            off = (extra[0] + 7) & ~7
            extra[0] = off + size
            assert extra[0] <= HEAP_BYTES, (off, size)
            return off

        SH1 = 4 * S + 4 * S + 6144 + 4096 + 4096
        SH3 = 31744
        if SH1 <= SZ_T and SH3 <= VST - QT:
            tail_base = 0
            fscr_base = 0
            ln_base = QT
        else:
            shared = alloc_extra(33000)
            tail_base = fscr_base = ln_base = shared

        RZBC = tail_base
        RZBC2 = RZBC + 4 * S
        EXPB_BASE = RZBC2 + 4 * S
        OT0 = EXPB_BASE + 6144
        OT1 = OT0 + 4096
        ZBUF = OT0

        if QT + 2 * NCD * D <= VST:
            WOT = QT
        else:
            WOT = alloc_extra(2 * NCD * D)

        F_XT = [fscr_base, fscr_base + 4096]
        F_AT = fscr_base + 8192
        F_HQT = fscr_base + 12288
        F_HQ = F_HQT + 2 * NI * 128
        F_U = [F_HQ + 2 * I, F_HQ + 2 * I + 2048]
        F_M = [F_U[1] + 2048, F_U[1] + 4096]
        F_QS = [F_M[1] + 2048, F_M[1] + 2560]
        assert F_QS[1] + 512 <= fscr_base + 33000, (F_QS,)

        L_X = [ln_base, ln_base + 4096]
        L_XN = [ln_base + 8192, ln_base + 12288]
        L_U = [ln_base + 16384, ln_base + 18432]
        L_M = [ln_base + 20480, ln_base + 22528]
        L_XNB = ln_base + 24576
        L_XNQ = [ln_base + 26624, ln_base + 28672]
        L_QS = [ln_base + 30720, ln_base + 31104]
        STATS = [ln_base + 31488, ln_base + 31616]

        wbase = F_WG
        W_STG = [wbase, wbase + 4096, wbase + 8192]
        W_QBF = [wbase + 12288, wbase + 14336]
        W_U = [wbase + 16384, wbase + 20480]
        W_M = [wbase + 24576, wbase + 28672]
        W_ST2 = [wbase + 32768, wbase + 34816]
        W_QS = [wbase + 36864, wbase + 37632]
        assert W_QS[1] + 768 <= HEAP_BYTES, (wbase,)



        ps_a = es.enter_context(nc.psum_tensor("ps_a", [128, 1024], F32))
        ps_b = es.enter_context(nc.psum_tensor("ps_b", [128, 1024], F32))
        ps_c = es.enter_context(nc.psum_tensor("ps_c", [128, 1024], F32))
        ps_t = es.enter_context(nc.psum_tensor("ps_t", [128, 1024], BF16))

        s_x = Sem(es, nc, "s_x")
        s_w = Sem(es, nc, "s_w")
        s_sc = Sem(es, nc, "s_sc")
        s_sc2 = Sem(es, nc, "s_sc2")
        s_zg = [Sem(es, nc, "s_zg0"), Sem(es, nc, "s_zg1")]
        s_st = Sem(es, nc, "s_st")
        s_ws = Sem(es, nc, "s_ws")
        s_pe = Sem(es, nc, "s_pe")
        s_ve = Sem(es, nc, "s_ve")
        s_ac = Sem(es, nc, "s_ac")
        s_po = Sem(es, nc, "s_po")
        s_wl = [Sem(es, nc, "s_wl0"), Sem(es, nc, "s_wl1"), Sem(es, nc, "s_wl2")]
        s_ws2 = [Sem(es, nc, "s_ws2a"), Sem(es, nc, "s_ws2b")]
        ALL = (s_x, s_w, s_sc, s_sc2, s_zg[0], s_zg[1], s_st, s_ws, s_pe, s_ve, s_ac, s_po,
               s_wl[0], s_wl[1], s_wl[2], s_ws2[0], s_ws2[1])

        P = Prog()

        def barrier(engines=("sync", "vector", "scalar", "tensor", "gpsimd")):
            snap = [(s, s.n) for s in ALL]
            for e in engines:
                P.wait(e, snap)

        P.emit("gpsimd", lambda e: e.memset(idn[:], 0.0), inc=(s_po, 1))
        P.emit("gpsimd", lambda e: e.memset(tri[:], 0.0), inc=(s_po, 1))
        P.emit("gpsimd", lambda e: e.affine_select(
            out=idn[:], in_=idn[:], pattern=[[-1, 128]], compare_op=ALU.not_equal,
            fill=1.0, base=0, channel_multiplier=1),
            waits=[(s_po, 2)], inc=(s_po, 1))
        P.emit("gpsimd", lambda e: e.affine_select(
            out=tri[:], in_=tri[:], pattern=[[-1, 128]], compare_op=ALU.is_gt,
            fill=1.0, base=0, channel_multiplier=1),
            waits=[(s_po, 2)], inc=(s_po, 1))
        P.emit("gpsimd", lambda e: e.memset(ceps[:], 1e-5), inc=(s_po, 1))
        po_const = s_po.n

        def quant_pre(src_f32, u_off, m_off, qs_off, F):
            nb = F // 16
            u, M = vf(u_off, F), vf(m_off, F)
            m_, sc_, r_ = vf(qs_off, nb), vf(qs_off + 4 * nb, nb), vf(qs_off + 8 * nb, nb)
            src3 = src_f32.rearrange("p (b w) -> p b w", w=16)
            P.emit("vector", lambda e: e.tensor_reduce(
                m_, src3, axis=AX.X, op=ALU.max, apply_absolute_value=True),
                waits=[(s_ve, s_ve.n)], inc=(s_ve, 1))
            P.emit("vector", lambda e: e.tensor_scalar(
                sc_, m_, SIXTH, 1e-12, op0=ALU.mult, op1=ALU.max),
                waits=[(s_ve, s_ve.n)], inc=(s_ve, 1))
            P.emit("vector", lambda e: e.reciprocal(r_, sc_),
                   waits=[(s_ve, s_ve.n)], inc=(s_ve, 1))
            rb = r_.unsqueeze(2).broadcast_to([128, nb, 16])
            P.emit("vector", lambda e: e.tensor_tensor(
                u.rearrange("p (b w) -> p b w", w=16), src3, rb, op=ALU.mult),
                waits=[(s_ve, s_ve.n)], inc=(s_ve, 1))
            P.emit("vector", lambda e: e.tensor_scalar(
                M.bitcast(U32), u.bitcast(U32), 0x7F800000, None, op0=ALU.bitwise_and),
                waits=[(s_ve, s_ve.n)], inc=(s_ve, 1))
            P.emit("vector", lambda e: e.tensor_scalar(
                M, M, MAGIC, MAGIC, op0=ALU.mult, op1=ALU.max),
                waits=[(s_ve, s_ve.n)], inc=(s_ve, 1))
            ve_m = s_ve.n
            P.emit("gpsimd", lambda e: e.tensor_tensor(u, u, M, op=ALU.add),
                   waits=[(s_ve, ve_m), (s_po, s_po.n)], inc=(s_po, 1))
            P.emit("gpsimd", lambda e: e.tensor_tensor(u, u, M, op=ALU.subtract),
                   waits=[(s_po, s_po.n)], inc=(s_po, 1))
            return (s_po.n, u_off, qs_off)

        def quant_deq(tok, out_bf, F, extra_pe_wait=0):
            po_q, u_off, qs_off = tok
            nb = F // 16
            u = vf(u_off, F)
            sc_ = vf(qs_off + 4 * nb, nb)
            scb = sc_.unsqueeze(2).broadcast_to([128, nb, 16])
            P.emit("vector", lambda e: e.tensor_tensor(
                out_bf.rearrange("p (b w) -> p b w", w=16),
                u.rearrange("p (b w) -> p b w", w=16), scb, op=ALU.mult),
                waits=[(s_po, po_q)] + ([(s_pe, extra_pe_wait)] if extra_pe_wait else []),
                inc=(s_ve, 1))
            return s_ve.n

        def quant(src_f32, out_bf, u_off, m_off, qs_off, F, extra_ve_wait=0):
            tok = quant_pre(src_f32, u_off, m_off, qs_off, F)
            return quant_deq(tok, out_bf, F, extra_pe_wait=extra_ve_wait)

        def bank_chunks(a, b):
            cuts = [a]
            nxt = (a // 512 + 1) * 512
            while nxt < b:
                cuts.append(nxt)
                nxt += 512
            cuts.append(b)
            return list(zip(cuts[:-1], cuts[1:]))

        wk_i = 0
        stg_rd = [0, 0]
        stgb_rd = [0, 0]
        st2_rd = [0]
        for win, wout in ((wqt_in, wq_d), (wkt_in, wk_d), (wvt_in, wv_d), (wot_in, wo_d)):
            for c in range(NCD):
                wp = wk_i % 2
                wk_i += 1
                stg, stgb = vf(W_STG[wp], D), vb(W_QBF[wp], D)
                P.emit("sync", lambda e, win=win, c=c, stg=stg: e.dma_start(
                    out=stg, in_=win[c * 128:(c + 1) * 128, :]),
                    waits=[(s_w, s_w.n), (s_ve, stg_rd[wp])], inc=(s_w, 16))
                P.emit("vector", lambda e, stg=stg, stgb=stgb: e.tensor_copy(stgb, stg),
                       waits=[(s_w, s_w.n), (s_ws, s_ws.n)], inc=(s_ve, 1))
                stg_rd[wp] = s_ve.n
                P.emit("sync", lambda e, wout=wout, c=c, stgb=stgb: e.dma_start(
                    out=wout[:, c, :], in_=stgb),
                    waits=[(s_ve, s_ve.n)], inc=(s_ws, 16))
        ws_attnW = s_ws.n
        P.wait("vector", [(s_ws, ws_attnW)])

        wq_items = []
        for win, wout in ((wg_in, wg_d), (wu_in, wu_d)):
            for ic in range(NI):
                wq_items.append((win, wout, "gu", ic, 0, D))
        for c in range(NCD):
            for p0 in range(0, I, 1024):
                wq_items.append((wd_in, wd_d, "wd", c, p0, min(1024, I - p0)))
        NWQ = len(wq_items)
        stg_free = [stg_rd[0], stg_rd[1], 0]
        load_tok = [0] * NWQ
        store_tok = [0] * NWQ
        deq_pend = [None] * NWQ

        def wq_load(j):
            win, wout, kind, c, p0, plen = wq_items[j]
            sl = j % 3
            sem = s_wl[sl]
            stg = vf(W_STG[sl], plen)
            P.emit("sync" if j % 2 == 0 else "scalar",
                   lambda e, win=win, c=c, p0=p0, plen=plen, stg=stg: e.dma_start(
                       out=stg, in_=win[c * 128:(c + 1) * 128, p0:p0 + plen]),
                   waits=[(sem, sem.n), (s_ve, stg_free[sl])], inc=(sem, 16))
            load_tok[j] = sem.n

        def wq_flush(k):
            win, wout, kind, c, p0, plen, tok = deq_pend[k]
            wp = k % 2
            g = plen // 128
            stgb = vb(W_QBF[wp], plen)
            ve_q = quant_deq(tok, stgb, plen, extra_pe_wait=stgb_rd[wp])
            for t in range(g):
                P.emit("tensor", lambda e, stgb=stgb, t=t: e.transpose(
                    ps_t[:, t * 128:(t + 1) * 128], stgb[:, t * 128:(t + 1) * 128], idn[:]),
                    waits=([(s_ve, ve_q), (s_ac, s_ac.n), (s_po, po_const)] if t == 0 else []),
                    inc=((s_pe, 1) if t == g - 1 else None))
            stgb_rd[wp] = s_pe.n
            stg2 = vb(W_ST2[wp], plen)
            ssem = s_ws2[wp]
            P.emit("scalar", lambda e, stg2=stg2, g=g: e.copy(stg2, ps_t[:, 0:g * 128]),
                   waits=[(s_pe, s_pe.n)] + ([(ssem, store_tok[k - 2])] if k >= 2 else []),
                   inc=(s_ac, 1))
            ac_e = s_ac.n
            if kind == "gu":
                P.emit("sync" if k % 2 == 0 else "scalar",
                       lambda e, wout=wout, c=c, stg2=stg2: e.dma_start(
                           out=wout[:, :, c * 128:(c + 1) * 128],
                           in_=stg2.rearrange("p (c i) -> p c i", i=128)),
                       waits=[(s_ac, ac_e), (ssem, ssem.n)], inc=(ssem, 16))
            else:
                t0 = p0 // 128
                P.emit("sync" if k % 2 == 0 else "scalar",
                       lambda e, t0=t0, c=c, g=g, stg2=stg2: e.dma_start(
                           out=wd_d[:, t0:t0 + g, c * 128:(c + 1) * 128],
                           in_=stg2.rearrange("p (t i) -> p t i", i=128)),
                       waits=[(s_ac, ac_e), (ssem, ssem.n)], inc=(ssem, 16))
            store_tok[k] = ssem.n

        for j in range(min(3, NWQ)):
            wq_load(j)
        for k in range(NWQ):
            win, wout, kind, c, p0, plen = wq_items[k]
            sl = k % 3
            P.wait("vector", [(s_wl[sl], load_tok[k])])
            ve0 = s_ve.n
            tok = quant_pre(vf(W_STG[sl], plen), W_U[k % 2], W_M[k % 2], W_QS[k % 2], plen)
            stg_free[sl] = ve0 + 4
            deq_pend[k] = (win, wout, kind, c, p0, plen, tok)
            if k + 3 < NWQ:
                wq_load(k + 3)
            if k >= 1:
                wq_flush(k - 1)
        wq_flush(NWQ - 1)
        ws_stageW2 = [s_ws2[0].n, s_ws2[1].n]

        for b in range(NB):
            barrier()

            wa_sb = vb(WA, NCD * D).rearrange("p (c n) -> p c n", n=D)
            wv_sb = vb(WVT, NCD * D).rearrange("p (c n) -> p c n", n=D)
            P.emit("sync", lambda e, wv_sb=wv_sb: e.dma_start(out=wv_sb, in_=wv_d[:]),
                   waits=[(s_ws, ws_attnW)], inc=(s_w, 16))
            P.emit("sync", lambda e, wa_sb=wa_sb: e.dma_start(out=wa_sb, in_=wq_d[:]),
                   inc=(s_w, 16))
            w_attn_ld = s_w.n

            xnt = vb(XNT, NCD * S).rearrange("p (c q) -> p c q", q=S)
            xnqt = vb(XNQT, NCD * S).rearrange("p (c q) -> p c q", q=S)
            vst = vb(VST, NT * 16 * 65).rearrange("p (t h n) -> p t h n", h=16, n=65)
            P.emit("vector", lambda e, vst=vst: e.memset(
                vst[:, :, :, 64].rearrange("p t h -> p (t h)"), 1.0), inc=(s_ve, 1))

            ac_vevac = 0
            ac_xnqt = []
            x_rd = [(0, 0), (0, 0)]
            xn_rd = [0, 0]
            st_rd = [0, 0]
            xnb_rd = 0
            xnq_rd = [0, 0]
            sqrt_ac = [0, 0]
            stats_t = {}

            def ln_xload(t):
                pp = t % 2
                xt = vf(L_X[pp], D)
                row0 = b * S + t * 128
                P.emit("sync", lambda e, xt=xt, row0=row0: e.dma_start(
                    out=xt, in_=x_in[row0:row0 + 128, :]),
                    waits=[(s_x, s_x.n), (s_ac, x_rd[pp][0]), (s_ve, x_rd[pp][1])],
                    inc=(s_x, 16))
                stats_t[t] = (xt, s_x.n)

            def ln_stats(t):
                pp = t % 2
                xt, x_ld = stats_t[t]
                sq = vf(L_XN[pp], D)
                P.emit("scalar", lambda e, xt=xt, sq=sq: e.activation(sq, xt, AF.Square),
                       waits=[(s_x, x_ld), (s_ve, xn_rd[pp])], inc=(s_ac, 1))
                ac_sq = s_ac.n
                stb = STATS[pp]
                s1, s2 = vf(stb, 1), vf(stb + 8, 1)
                mean, var = vf(stb + 16, 1), vf(stb + 24, 1)
                sd, rstd, nmb, m2 = (vf(stb + 32, 1), vf(stb + 40, 1),
                                     vf(stb + 48, 1), vf(stb + 56, 1))
                P.emit("vector", lambda e, xt=xt, s1=s1: e.reduce_sum(s1, xt, axis=AX.X),
                       waits=[(s_x, x_ld), (s_ve, s_ve.n), (s_ac, st_rd[pp])], inc=(s_ve, 1))
                P.emit("vector", lambda e, sq=sq, s2=s2: e.reduce_sum(s2, sq, axis=AX.X),
                       waits=[(s_ac, ac_sq), (s_ve, s_ve.n)], inc=(s_ve, 1))
                P.emit("vector", lambda e, s1=s1, mean=mean: e.tensor_scalar(
                    mean, s1, 1.0 / D, None, op0=ALU.mult),
                    waits=[(s_ve, s_ve.n)], inc=(s_ve, 1))
                P.emit("vector", lambda e, s2=s2, var=var: e.tensor_scalar(
                    var, s2, 1.0 / D, None, op0=ALU.mult),
                    waits=[(s_ve, s_ve.n)], inc=(s_ve, 1))
                P.emit("vector", lambda e, m2=m2, mean=mean: e.tensor_scalar(
                    m2, mean, mean, None, op0=ALU.mult),
                    waits=[(s_ve, s_ve.n)], inc=(s_ve, 1))
                P.emit("vector", lambda e, var=var, m2=m2: e.tensor_tensor(
                    var, var, m2, op=ALU.subtract),
                    waits=[(s_ve, s_ve.n)], inc=(s_ve, 1))
                ve_stats = s_ve.n
                P.emit("scalar", lambda e, sd=sd, var=var: e.activation(
                    sd, var, AF.Sqrt, bias=ceps[:]),
                    waits=[(s_ve, ve_stats), (s_po, po_const)], inc=(s_ac, 1))
                sqrt_ac[pp] = s_ac.n
                stats_t[t] = (xt, x_ld, sd, rstd, nmb, mean)

            ln_xload(0)
            ln_stats(0)
            ac_xnt_prev = 0
            for t in range(NT):
                pp = t % 2
                xt, x_ld, sd, rstd, nmb, mean = stats_t.pop(t)
                P.emit("vector", lambda e, rstd=rstd, sd=sd: e.reciprocal(rstd, sd),
                       waits=[(s_ac, sqrt_ac[pp]), (s_ve, s_ve.n)], inc=(s_ve, 1))
                P.emit("vector", lambda e, nmb=nmb, mean=mean, rstd=rstd: e.tensor_scalar(
                    nmb, mean, rstd, -1.0, op0=ALU.mult, op1=ALU.mult),
                    waits=[(s_ve, s_ve.n)], inc=(s_ve, 1))
                ve_st = s_ve.n
                xn = vf(L_XN[pp], D)
                P.emit("scalar", lambda e, xn=xn, xt=xt, rstd=rstd, nmb=nmb: e.activation(
                    xn, xt, AF.Identity, bias=nmb, scale=rstd),
                    waits=[(s_ve, ve_st)], inc=(s_ac, 1))
                ac_xn = s_ac.n
                xnb = vb(L_XNB, D)
                P.emit("scalar", lambda e, xnb=xnb, xt=xt, rstd=rstd, nmb=nmb: e.activation(
                    xnb, xt, AF.Identity, bias=nmb, scale=rstd),
                    waits=[(s_pe, xnb_rd)], inc=(s_ac, 1))
                ac_xnb = s_ac.n
                st_rd[pp] = ac_xnb
                x_rd[pp] = (ac_xnb, ve_st)
                if t + 1 < NT:
                    ln_xload(t + 1)
                xnq = vb(L_XNQ[pp], D)
                P.wait("vector", [(s_ac, ac_xn), (s_pe, xnq_rd[pp])])
                tok0 = quant_pre(xn[:, 0:512], L_U[0], L_M[0], L_QS[0], 512)
                tok1 = quant_pre(xn[:, 512:1024], L_U[1], L_M[1], L_QS[1], 512)
                quant_deq(tok0, xnq[:, 0:512], 512)
                ve_xnq = quant_deq(tok1, xnq[:, 512:1024], 512)
                xn_rd[pp] = ve_xnq
                if t >= 1:
                    pxnq, pt = pending_xnq
                    for c in range(NCD):
                        P.emit("tensor", lambda e, pxnq=pxnq, c=c: e.transpose(
                            ps_t[:, c * 128:(c + 1) * 128],
                            pxnq[:, c * 128:(c + 1) * 128], idn[:]),
                            waits=([(s_ve, pending_ve), (s_ac, ac_xnt_prev)]
                                   if c == 0 else []),
                            inc=((s_pe, 1) if c == NCD - 1 else None))
                    xnq_rd[pt % 2] = s_pe.n
                    P.emit("scalar", lambda e, pt=pt, xnqt=xnqt: e.copy(
                        xnqt[:, :, pt * 128:(pt + 1) * 128],
                        ps_t[:].rearrange("p (c q) -> p c q", q=128)),
                        waits=[(s_pe, s_pe.n)], inc=(s_ac, 1))
                    ac_xnqt.append(s_ac.n)
                pending_xnq = (xnq, t)
                pending_ve = ve_xnq
                ac_psf = ac_xnqt[-1] if t >= 1 else ac_xnb
                for c in range(NCD):
                    P.emit("tensor", lambda e, xnb=xnb, c=c: e.transpose(
                        ps_t[:, c * 128:(c + 1) * 128], xnb[:, c * 128:(c + 1) * 128], idn[:]),
                        waits=([(s_ac, ac_psf), (s_po, po_const)] if c == 0 else []),
                        inc=((s_pe, 1) if c == NCD - 1 else None))
                xnb_rd = s_pe.n
                P.emit("scalar", lambda e, t=t, xnt=xnt: e.copy(
                    xnt[:, :, t * 128:(t + 1) * 128],
                    ps_t[:].rearrange("p (c q) -> p c q", q=128)),
                    waits=[(s_pe, s_pe.n)], inc=(s_ac, 1))
                ac_xnt_t = s_ac.n
                ac_xnt_prev = ac_xnt_t
                for c in range(NCD):
                    for ci, (ca, cb) in enumerate(bank_chunks(0, D)):
                        P.emit("tensor", lambda e, t=t, c=c, ca=ca, cb=cb, xnt=xnt,
                               wv_sb=wv_sb: e.matmul(
                            ps_b[:, ca:cb], xnt[:, c, t * 128:(t + 1) * 128],
                            wv_sb[:, c, ca:cb],
                            start=(c == 0), stop=(c == NCD - 1)),
                            waits=([(s_ac, ac_xnt_t), (s_w, w_attn_ld)]
                                   if (c == 0 and ci == 0) else []),
                            inc=((s_pe, 1) if (c == NCD - 1 and cb == D) else None))
                P.emit("scalar", lambda e, t=t, vst=vst: e.copy(
                    vst[:, t, :, 0:64], ps_b[:].rearrange("p (h v) -> p h v", v=64)),
                    waits=[(s_pe, s_pe.n)], inc=(s_ac, 1))
                ac_vevac = s_ac.n
                if t + 1 < NT:
                    ln_stats(t + 1)
            pxnq, pt = pending_xnq
            for c in range(NCD):
                P.emit("tensor", lambda e, pxnq=pxnq, c=c: e.transpose(
                    ps_t[:, c * 128:(c + 1) * 128],
                    pxnq[:, c * 128:(c + 1) * 128], idn[:]),
                    waits=([(s_ve, pending_ve), (s_ac, s_ac.n)] if c == 0 else []),
                    inc=((s_pe, 1) if c == NCD - 1 else None))
            xnq_rd[pt % 2] = s_pe.n
            P.emit("scalar", lambda e, pt=pt, xnqt=xnqt: e.copy(
                xnqt[:, :, pt * 128:(pt + 1) * 128],
                ps_t[:].rearrange("p (c q) -> p c q", q=128)),
                waits=[(s_pe, s_pe.n)], inc=(s_ac, 1))
            ac_xnqt.append(s_ac.n)

            qt_sb = vb(QT, NP * S).rearrange("p (r q) -> p r q", q=S)
            kt_sb = vb(KT, NP * S).rearrange("p (r q) -> p r q", q=S)
            evac_of_buf = {0: ac_vevac, 1: ac_vevac}
            gidx = 0
            w_ld = w_attn_ld
            for widx, dst in enumerate((qt_sb, kt_sb)):
                if widx == 1:
                    P.emit("sync", lambda e, wa_sb=wa_sb: e.dma_start(
                        out=wa_sb, in_=wk_d[:]),
                        waits=[(s_pe, s_pe.n)], inc=(s_w, 16))
                    w_ld = s_w.n
                for p in range(NP):
                    for qc in range(NQC):
                        bi = gidx % 2
                        ps = ps_a if bi == 0 else ps_b
                        gidx += 1
                        for c in range(NCD):
                            for ci, (ca, cb) in enumerate(bank_chunks(0, QC)):
                                P.emit("tensor", lambda e, ps=ps, c=c, p=p, qc=qc,
                                       ca=ca, cb=cb: e.matmul(
                                    ps[:, ca:cb], wa_sb[:, c, p * 128:(p + 1) * 128],
                                    xnt[:, c, qc * QC + ca:qc * QC + cb],
                                    start=(c == 0), stop=(c == NCD - 1)),
                                    waits=([(s_ac, max(evac_of_buf[bi], ac_xnqt[-1])),
                                            (s_w, w_ld)] if (c == 0 and ci == 0) else []),
                                    inc=((s_pe, 1) if (c == NCD - 1 and cb == QC) else None))
                        P.emit("scalar", lambda e, dst=dst, p=p, qc=qc, ps=ps: e.copy(
                            dst[:, p, qc * QC:(qc + 1) * QC], ps[:, 0:QC]),
                            waits=[(s_pe, s_pe.n)], inc=(s_ac, 1))
                        evac_of_buf[bi] = s_ac.n
            pe_qkv_done = s_pe.n

            atnt = vb(ATNT, NP * S).rearrange("p (r q) -> p r q", q=S)
            zbuf = vf(ZBUF, S // 8, parts=128)
            ws_zg2 = [0, 0]
            QCH = min(512, QC)
            NQCH = S // QCH
            scbuf = [ps_a[:, 0:512], ps_a[:, 512:1024], ps_b[:, 0:512], ps_b[:, 512:1024]]
            pvps = [ps_c[0:65, 0:512], ps_c[0:65, 512:1024]]
            pvreader4 = [evac_of_buf[0], evac_of_buf[0], evac_of_buf[1], evac_of_buf[1]]
            pc_rd = [(0, 0), (0, 0)]
            for pr in range(NP):
                for qc in range(NQCH):
                    jmax = (qc + 1) * QCH // 128
                    sc_pe = {}
                    for jj in range(0, jmax, 2):
                        js = [j for j in range(jj, min(jj + 2, jmax))]
                        for j in js:
                            q0 = max(qc * QCH, j * 128)
                            n = (qc + 1) * QCH - q0
                            for s_ in (0, 1):
                                ps = scbuf[(j % 2) * 2 + s_]
                                P.emit("tensor", lambda e, ps=ps, pr=pr, s_=s_, j=j,
                                       q0=q0, n=n, kt_sb=kt_sb, qt_sb=qt_sb: e.matmul(
                                    ps[:, 0:n],
                                    kt_sb[s_ * 64:(s_ + 1) * 64, pr, j * 128:(j + 1) * 128],
                                    qt_sb[s_ * 64:(s_ + 1) * 64, pr, q0:q0 + n],
                                    start=True, stop=True),
                                    waits=[(s_ac, pvreader4[(j % 2) * 2 + s_])],
                                    inc=(s_pe, 1))
                                sc_pe[(j, s_)] = s_pe.n
                        for j in js:
                            q0 = max(qc * QCH, j * 128)
                            n = (qc + 1) * QCH - q0
                            off = q0 - qc * QCH
                            for s_ in (0, 1):
                                ps = scbuf[(j % 2) * 2 + s_]
                                eb = vb(EXPB_BASE + 2 * QCH * ((j % 3) * 2 + s_), QCH)
                                P.emit("scalar", lambda e, eb=eb, ps=ps, n=n: e.activation(
                                    eb[:, 0:n], ps[:, 0:n], AF.Exp, scale=0.125),
                                    waits=[(s_pe, sc_pe[(j, s_)])], inc=(s_ac, 1))
                                ac_exp = s_ac.n
                                pvreader4[(j % 2) * 2 + s_] = ac_exp
                                po_mask = 0
                                if j * 128 >= qc * QCH:
                                    P.emit("gpsimd", lambda e, eb=eb: e.tensor_tensor(
                                        eb[:, 0:128], eb[:, 0:128], tri[:], op=ALU.mult),
                                        waits=[(s_ac, ac_exp), (s_po, s_po.n)], inc=(s_po, 1))
                                    po_mask = s_po.n
                                pv = pvps[s_]
                                P.emit("tensor", lambda e, j=j, eb=eb, off=off, n=n, pv=pv,
                                       pr=pr, s_=s_, vst=vst, jm=jmax: e.matmul(
                                    pv[:, off:off + n],
                                    vst[:, j, pr * 2 + s_, 0:65], eb[:, 0:n],
                                    start=(j == 0), stop=(j == jm - 1)),
                                    waits=[(s_ac, ac_exp)]
                                          + ([(s_po, po_mask)] if po_mask else [])
                                          + ([(s_ve, pc_rd[s_][0]),
                                              (s_ac, pc_rd[s_][1])] if j == 0 else []),
                                    inc=(s_pe, 1))
                    pe_pv = s_pe.n
                    for s_ in (0, 1):
                        h = pr * 2 + s_
                        pv = pvps[s_]
                        P.emit("vector", lambda e, pr=pr, s_=s_, qc=qc, atnt=atnt, pv=pv: e.tensor_copy(
                            atnt[s_ * 64:(s_ + 1) * 64, pr, qc * QCH:(qc + 1) * QCH],
                            pv[0:64, 0:QCH]),
                            waits=[(s_pe, pe_pv), (s_ve, s_ve.n)], inc=(s_ve, 1))
                        ve_pc = s_ve.n
                        ztmp = vf(RZBC if s_ == 0 else RZBC2, S, parts=1)
                        P.emit("scalar", lambda e, qc=qc, ztmp=ztmp, pv=pv: e.copy(
                            ztmp[0:1, qc * QCH:(qc + 1) * QCH], pv[64:65, 0:QCH]),
                            waits=[(s_pe, pe_pv)]
                                  + ([(s_zg[s_], ws_zg2[s_])] if (qc == 0 and ws_zg2[s_]) else []),
                            inc=(s_ac, 1))
                        pc_rd[s_] = (ve_pc, s_ac.n)
                for s_ in (0, 1):
                    h = pr * 2 + s_
                    ztmp = vf(RZBC if s_ == 0 else RZBC2, S, parts=1)
                    P.emit("scalar", lambda e, h=h, ztmp=ztmp: e.dma_start(
                        out=rz_d[h:h + 1, :], in_=ztmp[0:1, :]),
                        waits=[(s_ac, s_ac.n)], inc=(s_zg[s_], 16))
                    ws_zg2[s_] = s_zg[s_].n
            pe_attn_done = s_pe.n

            wo_sb = vb(WOT, NCD * D).rearrange("p (c n) -> p c n", n=D)
            P.emit("sync", lambda e, wo_sb=wo_sb: e.dma_start(out=wo_sb, in_=wo_d[:]),
                   waits=[(s_pe, pe_attn_done)], inc=(s_w, 16))
            w_wo_ld = s_w.n
            wgq = vb(F_WG, NCD * I).rearrange("p (c i) -> p c i", i=I)
            wuq = vb(F_WU, NCD * I).rearrange("p (c i) -> p c i", i=I)
            P.emit("sync", lambda e, wgq=wgq: e.dma_start(out=wgq, in_=wg_d[:]),
                   waits=[(s_ws2[0], ws_stageW2[0]), (s_ws2[1], ws_stageW2[1]),
                          (s_w, s_w.n)], inc=(s_w, 16))
            P.emit("sync", lambda e, wuq=wuq: e.dma_start(out=wuq, in_=wu_d[:]),
                   inc=(s_w, 16))
            w_gu_ld = s_w.n

            P.emit("scalar", lambda e, zbuf=zbuf: e.dma_start(
                out=zbuf[:, :], in_=rz_d.rearrange("h (a c) -> (h a) c", a=8)),
                waits=[(s_zg[0], ws_zg2[0]), (s_zg[1], ws_zg2[1])], inc=(s_ws, 16))
            ws_zld = s_ws.n
            P.emit("vector", lambda e, zbuf=zbuf: e.reciprocal(zbuf[:, :], zbuf[:, :]),
                   waits=[(s_ws, ws_zld), (s_ve, s_ve.n)], inc=(s_ve, 1))
            ve_rz = s_ve.n
            P.emit("scalar", lambda e, zbuf=zbuf: e.dma_start(
                out=rz_d.rearrange("h (a c) -> (h a) c", a=8), in_=zbuf[:, :]),
                   waits=[(s_ve, ve_rz)], inc=(s_ws, 16))
            ws_rz = s_ws.n
            ve_norm_p = []
            sc_hist = [0, 0]
            sel2 = vf(EXPB_BASE, 128, parts=2)
            P.emit("gpsimd", lambda e, sel2=sel2: e.memset(sel2[:], 1.0),
                   waits=[(s_pe, pe_attn_done), (s_po, s_po.n)], inc=(s_po, 1))
            P.emit("gpsimd", lambda e, sel2=sel2: e.affine_select(
                out=sel2[:], in_=sel2[:], pattern=[[1, 128]], compare_op=ALU.is_ge,
                fill=0.0, base=0, channel_multiplier=-64),
                waits=[(s_po, s_po.n)], inc=(s_po, 1))
            P.emit("gpsimd", lambda e, sel2=sel2: e.affine_select(
                out=sel2[:], in_=sel2[:], pattern=[[-1, 128]], compare_op=ALU.is_ge,
                fill=0.0, base=63, channel_multiplier=64),
                waits=[(s_po, s_po.n)], inc=(s_po, 1))
            po_sel = s_po.n
            ve_norm_last = 0
            for p in range(NP):
                zpair = vf(RZBC if p % 2 == 0 else RZBC2, S, parts=2)
                ssem = s_sc if p % 2 == 0 else s_sc2
                P.emit("scalar", lambda e, p=p, zpair=zpair: e.dma_start(
                    out=zpair[0:2, :], in_=rz_d[2 * p:2 * p + 2, :]),
                    waits=[(s_ws, ws_rz)]
                          + ([(s_ve, sc_hist[p % 2])] if sc_hist[p % 2] else []),
                    inc=(ssem, 16))
                sc_rz = (ssem, ssem.n)
                HWID = min(1024, S)
                CWID = min(512, S)
                for half in range(S // HWID):
                    nchunk = HWID // CWID
                    for ci in range(nchunk):
                        P.emit("tensor", lambda e, zpair=zpair, half=half, ci=ci,
                               HWID=HWID, CWID=CWID: e.matmul(
                            ps_c[:, ci * CWID:(ci + 1) * CWID], sel2[0:2, :],
                            zpair[0:2, half * HWID + ci * CWID: half * HWID + (ci + 1) * CWID],
                            start=True, stop=True),
                            waits=([sc_rz, (s_po, po_sel)]
                                   + ([(s_ve, s_ve.n), (s_ac, s_ac.n)]
                                      if (p == 0 and half == 0) else
                                      [(s_ve, ve_norm_last)])
                                   if ci == 0 else []),
                            inc=((s_pe, 1) if ci == nchunk - 1 else None))
                    pe_bc = s_pe.n
                    P.emit("vector", lambda e, p=p, half=half, atnt=atnt, HWID=HWID: e.tensor_tensor(
                        atnt[:, p, half * HWID:(half + 1) * HWID],
                        atnt[:, p, half * HWID:(half + 1) * HWID],
                        ps_c[:, 0:HWID], op=ALU.mult),
                        waits=[(s_pe, pe_bc), (s_ve, s_ve.n)], inc=(s_ve, 1))
                    ve_norm_last = s_ve.n
                sc_hist[p % 2] = s_ve.n
                ve_norm_p.append(s_ve.n)
            ve_norm = s_ve.n

            ws_ot_last = 0
            evac_a = max(pvreader4[0], pvreader4[1])
            for t in range(NT):
                for p in range(NP):
                    for ci, (ca, cb) in enumerate(bank_chunks(0, D)):
                        w0 = []
                        if ci == 0:
                            if t == 0:
                                w0 = [(s_ve, ve_norm_p[p]), (s_w, w_wo_ld)]
                                if p == 0:
                                    w0.append((s_ac, evac_a))
                            elif p == 0:
                                w0 = [(s_ac, evac_a)]
                        P.emit("tensor", lambda e, t=t, p=p, ca=ca, cb=cb, atnt=atnt,
                               wo_sb=wo_sb: e.matmul(
                            ps_a[:, ca:cb], atnt[:, p, t * 128:(t + 1) * 128],
                            wo_sb[:, p, ca:cb],
                            start=(p == 0), stop=(p == NP - 1)),
                            waits=w0,
                            inc=((s_pe, 1) if (p == NP - 1 and cb == D) else None))
                pe_op = s_pe.n
                ot = vf(OT0 if t % 2 == 0 else OT1, D)
                P.emit("scalar", lambda e, ot=ot: e.copy(ot, ps_a[:]),
                       waits=[(s_pe, pe_op)] + ([(s_ws, ws_ot_last)] if (t >= 2 and ws_ot_last) else []),
                       inc=(s_ac, 1))
                evac_a = s_ac.n
                P.emit("sync", lambda e, ot=ot, b=b, t=t: e.dma_start(
                    out=attn_scr[b * S + t * 128: b * S + (t + 1) * 128, :], in_=ot),
                    waits=[(s_ac, s_ac.n)], inc=(s_ws, 16))
                ws_ot_last = s_ws.n
            pe_outproj_done = s_pe.n
            ws_attn = s_ws.n

            barrier()
            wdq = vb(F_WD, NI * D).rearrange("p (t n) -> p t n", n=D)
            P.emit("sync", lambda e, wdq=wdq: e.dma_start(out=wdq, in_=wd_d[:]),
                   waits=[(s_ws2[0], ws_stageW2[0]), (s_ws2[1], ws_stageW2[1]),
                          (s_pe, pe_outproj_done)], inc=(s_sc2, 16))
            w_d_ld = s_sc2.n


            NICH = (I + ICH - 1) // ICH
            hbuf = vf(F_H, I)
            hq = vb(F_HQ, I)
            hqt = vb(F_HQT, NI * 128)
            silu_ac = [0, 0]
            hmul_ve = [0, 0]
            hbuf_rd = [0] * NICH
            st_xt = [0, 0]
            po_f_prev = 0
            tr_pe_done = 0
            ac_tr_all = 0
            ve_f_prev = 0
            prev = None
            for t in range(NT + 1):
                pp = t % 2
                if t < NT:
                    row0 = b * S + t * 128
                    xt = vf(F_XT[pp], D)
                    P.emit("sync", lambda e, xt=xt, row0=row0: e.dma_start(
                        out=xt, in_=x_in[row0:row0 + 128, :]),
                        waits=[(s_x, s_x.n)] + ([(s_st, st_xt[pp])] if st_xt[pp] else []),
                        inc=(s_x, 16))
                    x_ld = s_x.n
                    pend_deq = None
                    for ici in range(NICH):
                        ip = ici % 2
                        psg = ps_a if ip == 0 else ps_b
                        i0 = ici * ICH
                        ilen = min(ICH, I - i0)
                        gw = [(s_ac, silu_ac[ip])] + ([(s_w, w_gu_ld)] if (t == 0 and ici == 0) else [])
                        for c in range(NCD):
                            P.emit("tensor", lambda e, t=t, c=c, i0=i0, ilen=ilen, psg=psg,
                                   xnqt=xnqt, wgq=wgq: e.matmul(
                                psg[:, 0:ilen], xnqt[:, c, t * 128:(t + 1) * 128],
                                wgq[:, c, i0:i0 + ilen], start=(c == 0), stop=(c == NCD - 1)),
                                waits=(gw if c == 0 else []),
                                inc=((s_pe, 1) if c == NCD - 1 else None))
                        pe_g = s_pe.n
                        for c in range(NCD):
                            P.emit("tensor", lambda e, t=t, c=c, i0=i0, ilen=ilen, psg=psg,
                                   xnqt=xnqt, wuq=wuq: e.matmul(
                                psg[:, 512:512 + ilen], xnqt[:, c, t * 128:(t + 1) * 128],
                                wuq[:, c, i0:i0 + ilen], start=(c == 0), stop=(c == NCD - 1)),
                                waits=([(s_ve, hmul_ve[ip])] if c == 0 else []),
                                inc=((s_pe, 1) if c == NCD - 1 else None))
                        pe_u = s_pe.n
                        P.emit("scalar", lambda e, i0=i0, ilen=ilen, psg=psg, hbuf=hbuf: e.activation(
                            hbuf[:, i0:i0 + ilen], psg[:, 0:ilen], AF.Silu),
                            waits=[(s_pe, pe_u)] + ([(s_ve, hbuf_rd[ici])] if hbuf_rd[ici] else []),
                            inc=(s_ac, 1))
                        ac_si = s_ac.n
                        silu_ac[ip] = ac_si
                        P.emit("vector", lambda e, i0=i0, ilen=ilen, psg=psg, hbuf=hbuf: e.tensor_tensor(
                            hbuf[:, i0:i0 + ilen], hbuf[:, i0:i0 + ilen],
                            psg[:, 512:512 + ilen], op=ALU.mult),
                            waits=[(s_ac, ac_si), (s_pe, pe_u), (s_ve, s_ve.n)], inc=(s_ve, 1))
                        hmul_ve[ip] = s_ve.n
                        tok = quant_pre(hbuf[:, i0:i0 + ilen], F_U[ip], F_M[ip],
                                        F_QS[ip], ilen)
                        if pend_deq is not None:
                            pi0, pilen, ptok = pend_deq
                            hbuf_rd[ici - 1] = quant_deq(
                                ptok, hq[:, pi0:pi0 + pilen], pilen,
                                extra_pe_wait=tr_pe_done)
                        pend_deq = (i0, ilen, tok)
                    pi0, pilen, ptok = pend_deq
                    hbuf_rd[NICH - 1] = quant_deq(
                        ptok, hq[:, pi0:pi0 + pilen], pilen, extra_pe_wait=tr_pe_done)
                    ve_hq = s_ve.n
                if prev is not None:
                    pxt, px_ld, prow0 = prev
                    pat = vf(F_AT, D)
                    P.emit("sync", lambda e, pat=pat, prow0=prow0: e.dma_start(
                        out=pat, in_=attn_scr[prow0:prow0 + 128, :]),
                        waits=[(s_sc, s_sc.n)]
                              + ([(s_ws, ws_attn)] if t == 1 else
                                 ([(s_po, po_f_prev)] if po_f_prev else [])),
                        inc=(s_sc, 16))
                    pat_ld = s_sc.n
                    for it in range(NI):
                        for ci, (ca, cb) in enumerate(bank_chunks(0, D)):
                            P.emit("tensor", lambda e, it=it, ca=ca, cb=cb, hqt=hqt,
                                   wdq=wdq: e.matmul(
                                ps_c[:, ca:cb], hqt[:, it * 128:(it + 1) * 128],
                                wdq[:, it, ca:cb],
                                start=(it == 0), stop=(it == NI - 1)),
                                waits=([(s_ac, ac_tr_all), (s_sc2, w_d_ld)]
                                       + ([(s_ve, ve_f_prev)] if ve_f_prev else [])
                                       if (it == 0 and ci == 0) else []),
                                inc=((s_pe, 1) if (it == NI - 1 and cb == D) else None))
                    pe_dn = s_pe.n
                    P.emit("vector", lambda e, pat=pat: e.tensor_tensor(
                        pat, ps_c[:], pat, op=ALU.add),
                        waits=[(s_pe, pe_dn), (s_sc, pat_ld), (s_ve, s_ve.n)], inc=(s_ve, 1))
                    ve_f = s_ve.n
                    ve_f_prev = ve_f
                    P.emit("gpsimd", lambda e, pat=pat, pxt=pxt: e.tensor_tensor(
                        pxt, pat, pxt, op=ALU.add),
                        waits=[(s_ve, ve_f), (s_x, px_ld), (s_po, s_po.n)], inc=(s_po, 1))
                    po_f = s_po.n
                    po_f_prev = po_f
                    P.emit("sync", lambda e, pxt=pxt, prow0=prow0: e.dma_start(
                        out=out_hbm[prow0:prow0 + 128, :], in_=pxt),
                        waits=[(s_po, po_f)], inc=(s_st, 16))
                    st_xt[(t - 1) % 2] = s_st.n
                if t < NT:
                    done = 0
                    while done < NI:
                        g = min(8, NI - done)
                        for k in range(g):
                            it = done + k
                            P.emit("tensor", lambda e, it=it, k=k, hq=hq: e.transpose(
                                ps_t[:, k * 128:(k + 1) * 128],
                                hq[:, it * 128:(it + 1) * 128], idn[:]),
                                waits=([(s_ve, ve_hq), (s_ac, s_ac.n)] if k == 0 else []),
                                inc=((s_pe, 1) if k == g - 1 else None))
                        P.emit("scalar", lambda e, done=done, g=g, hqt=hqt: e.copy(
                            hqt[:, done * 128:(done + g) * 128], ps_t[:, 0:g * 128]),
                            waits=[(s_pe, s_pe.n)], inc=(s_ac, 1))
                        done += g
                    ac_tr_all = s_ac.n
                    tr_pe_done = s_pe.n
                    prev = (xt, x_ld, row0)

        P.wait("sync", [(s_st, s_st.n)])

        with nc.Block() as block:
            @block.sync
            def _(e):
                for f in P.ops["sync"]:
                    f(e)

            @block.vector
            def _(e):
                for f in P.ops["vector"]:
                    f(e)

            @block.scalar
            def _(e):
                for f in P.ops["scalar"]:
                    f(e)

            @block.tensor
            def _(e):
                for f in P.ops["tensor"]:
                    f(e)

            @block.gpsimd
            def _(e):
                for f in P.ops["gpsimd"]:
                    f(e)

    return nc


def kernel(L0_attn_x, L0_attn_WQ, L0_attn_WK, L0_attn_WV, L0_attn_WO,
           L0_ffn_w_gate, L0_ffn_w_up, L0_ffn_w_down):
    x = np.ascontiguousarray(np.asarray(L0_attn_x, dtype=np.float32))
    B, S, D = x.shape
    NB = B // N_CORES
    wqt = np.ascontiguousarray(np.asarray(L0_attn_WQ, np.float32).transpose(2, 0, 1).reshape(D, D))
    wkt = np.ascontiguousarray(np.asarray(L0_attn_WK, np.float32).transpose(2, 0, 1).reshape(D, D))
    wvt = np.ascontiguousarray(np.asarray(L0_attn_WV, np.float32).transpose(2, 0, 1).reshape(D, D))
    wot = np.ascontiguousarray(np.asarray(L0_attn_WO, np.float32).transpose(0, 2, 1).reshape(D, D))
    wg = np.ascontiguousarray(np.asarray(L0_ffn_w_gate, np.float32))
    wu = np.ascontiguousarray(np.asarray(L0_ffn_w_up, np.float32))
    wd = np.ascontiguousarray(np.asarray(L0_ffn_w_down, np.float32))

    nc = bass.Bass("TRN2", target_bir_lowering=False, debug=False)
    build_core_program(nc, S=S, NB=NB)

    in_maps = []
    for core in range(N_CORES):
        shard = x[core * NB:(core + 1) * NB].reshape(NB * S, D)
        in_maps.append({
            "x": shard, "wqt": wqt, "wkt": wkt, "wvt": wvt, "wot": wot,
            "w_gate": wg, "w_up": wu, "w_down": wd,
        })
    res = run_bass_kernel_spmd(nc, in_maps, core_ids=list(range(N_CORES)))
    out = np.concatenate([r["out"].reshape(NB, S, D) for r in res.results], axis=0)
    return out.astype(np.float32)
